# version 10
# baseline (speedup 1.0000x reference)
import numpy as np
import ml_dtypes
import concourse.bass as bass
import concourse.mybir as mybir
from concourse.bass_utils import run_bass_kernel_spmd

F32 = mybir.dt.float32
BF16 = mybir.dt.bfloat16
I32 = mybir.dt.int32
AX = mybir.AxisListType.X
ALU = mybir.AluOpType
ACT = mybir.ActivationFunctionType

P = 128
D = 1024
DH = 4096
HC = D // P
CC = DH // P
GAMMA = 0.96875
EPS = 1e-5
NCORES = 8


class Em:

    ENGINES = ("sync", "gpsimd", "tensor", "vector", "scalar")

    def __init__(self, xdv=()):
        self.xdv = set(xdv)
        self.region = "pro"
        self.q = {e: [] for e in self.ENGINES}
        self.cnt = {}
        self.esem = {}
        self.wr = {}
        self.rd = {}
        self.psum_res = set()

    def _xtra(self, eng=None):
        glue = "glue" in self.xdv and eng != "tensor"
        ls = ("ls_" + self.region) in self.xdv or "ls" in self.xdv
        out = ()
        if glue or ls:
            out = tuple((sm, self.cnt[sm[1]]) for sm in self.esem.values())
            for pl in self.pool.values():
                out = out + tuple((sm, self.cnt[sm[1]]) for sm in pl)
        elif eng == "tensor":
            if "gt_d" in self.xdv:
                for pl in self.pool.values():
                    out = out + tuple((sm, self.cnt[sm[1]]) for sm in pl)
            if "gt_e" in self.xdv:
                out = out + tuple((sm, self.cnt[sm[1]])
                                  for sm in self.esem.values())
        return out

    def sem(self, handle, name):
        self.cnt[name] = 0
        return (handle, name)

    def val(self, sem):
        return self.cnt[sem[1]]

    def _raw_op(self, eng, fn, waits, incs):
        for (s, v) in waits:
            assert v <= self.cnt[s[1]], \
                f"forward wait {s[1]}>={v} emitted={self.cnt[s[1]]}"
        for (s, n) in incs:
            self.cnt[s[1]] += n
        self.q[eng].append((fn, tuple(waits), tuple(incs)))

    def _collect(self, own, reads, writes, waits):
        w = {}

        def add(tok):
            if tok is None:
                return
            s, v = tok
            if own is not None and s[1] == own[1]:
                return
            if s[1] in w:
                w[s[1]] = (s, max(v, w[s[1]][1]))
            else:
                w[s[1]] = (s, v)

        for r in reads:
            add(self.wr.get(r))
            if r in self.psum_res:
                for tok in self.rd.get(r, {}).values():
                    add(tok)
        for r in writes:
            add(self.wr.get(r))
            for tok in self.rd.get(r, {}).values():
                add(tok)
        for tok in waits:
            add(tok)
        return list(w.values())

    def _post(self, tok, reads, writes, key):
        for r in writes:
            self.wr[r] = tok
            self.rd[r] = {}
        for r in reads:
            self.rd.setdefault(r, {})[key] = tok

    def op(self, eng, fn, reads=(), writes=(), waits=()):
        es = self.esem[eng]
        waits = tuple(waits) + self._xtra(eng)
        w = self._collect(None, reads, writes, waits)
        self._raw_op(eng, fn, w, [(es, 1)])
        tok = (es, self.cnt[es[1]])
        self._post(tok, reads, writes, es[1])
        return tok

    def set_pool(self, eng, sems):
        if not hasattr(self, "pool"):
            self.pool = {}
            self.pool_n = {}
        self.pool[eng] = sems
        self.pool_n[eng] = 0

    def dma(self, eng, out_ap, in_ap, reads=(), writes=(), waits=()):
        pool = self.pool[eng]
        n = self.pool_n[eng]
        sm = pool[n % len(pool)]
        prev = 16 * (n // len(pool))
        waits = tuple(waits) + self._xtra(eng)
        w = self._collect(None, reads, writes,
                          tuple(waits) + ((sm, prev),))
        self._raw_op(eng, lambda e, o=out_ap, i=in_ap:
                     e.dma_start(out=o, in_=i), w, [(sm, 16)])
        self.pool_n[eng] = n + 1
        tok = (sm, prev + 16)
        self._post(tok, reads, writes, sm[1])
        return tok

    def dma_all(self, eng):
        return [(sm, self.cnt[sm[1]]) for sm in self.pool[eng]]

    def emit(self, block):
        def body(eng_obj, name):
            hwm = {}
            for (fn, waits, incs) in self.q[name]:
                for (s, v) in waits:
                    if v > 0 and v > hwm.get(s[1], 0):
                        eng_obj.wait_ge(s[0], v)
                        hwm[s[1]] = v
                ins = fn(eng_obj) if fn is not None else None
                for (s, n) in incs:
                    if ins is None:
                        eng_obj.sem_inc(s[0], n)
                    else:
                        ins.then_inc(s[0], n)
        block.sync(lambda e: body(e, "sync"))
        block.gpsimd(lambda e: body(e, "gpsimd"))
        block.tensor(lambda e: body(e, "tensor"))
        block.vector(lambda e: body(e, "vector"))
        block.scalar(lambda e: body(e, "scalar"))


def build_nc(NSEQ, S, xdv=()):
    NT = S // P
    NW = max(1, S // 512)
    WIN = S // NW
    WT = WIN // P
    NBLK = max(1, S // 512)
    BLK = S // NBLK
    BT = BLK // P

    nc = bass.Bass("TRN2", target_bir_lowering=False,
                   detect_race_conditions=False,
                   dynamic_dma_scratch_size=8192)

    x_h = nc.dram_tensor("x", [NSEQ, S, D], F32, kind="ExternalInput")
    y_h = nc.dram_tensor("y", [NSEQ, S, D], F32, kind="ExternalOutput")
    wq_h = nc.dram_tensor("wq", [P, HC * D], F32, kind="ExternalInput")
    wk_h = nc.dram_tensor("wk", [P, HC * D], F32, kind="ExternalInput")
    wv_h = nc.dram_tensor("wv", [P, HC * D], F32, kind="ExternalInput")
    bqk_h = nc.dram_tensor("bqk", [P, 16], F32, kind="ExternalInput")
    w1_h = nc.dram_tensor("w1", [P, CC * D], F32, kind="ExternalInput")
    w2_h = nc.dram_tensor("w2", [P, HC * DH], F32, kind="ExternalInput")
    cw_h = nc.dram_tensor("cw", [P, 3 * CC], F32, kind="ExternalInput")
    cb_h = nc.dram_tensor("cb", [P, CC], F32, kind="ExternalInput")
    idb_h = nc.dram_tensor("idb", [P, P], BF16, kind="ExternalInput")
    idf_h = nc.dram_tensor("idf", [P, P], F32, kind="ExternalInput")
    mk_h = nc.dram_tensor("msk", [P, P], BF16, kind="ExternalInput")
    onf_h = nc.dram_tensor("onf", [1, P], F32, kind="ExternalInput")
    on8_h = nc.dram_tensor("on8", [8, P], F32, kind="ExternalInput")
    onc_h = nc.dram_tensor("onc", [P, 1], F32, kind="ExternalInput")
    cst_h = nc.dram_tensor("cst", [P, 8], F32, kind="ExternalInput")

    aw_s = nc.dram_tensor("aw_s", [3, P, HC * D], BF16, kind="Internal")
    w1_s = nc.dram_tensor("w1_s", [CC // 4, P, HC * 512], BF16,
                          kind="Internal")
    w2_s = nc.dram_tensor("w2_s", [HC, P, CC * P], BF16, kind="Internal")

    def scr_ap(t, row, ncols, coff=0, ccnt=None):
        return bass.AP(t, row * P * ncols + coff,
                       [[ncols, P], [1, ccnt if ccnt is not None else ncols]])

    from contextlib import ExitStack
    ctx = ExitStack()
    sb = lambda n, s, d: ctx.enter_context(nc.sbuf_tensor(n, s, d))
    psf = lambda n, s: ctx.enter_context(nc.psum_tensor(n, s, F32))

    kT = sb("kT", [P, HC * S], BF16)
    vT = sb("vT", [P, NT * D], BF16)
    x2 = sb("x2", [P, NT * D], BF16)
    hq = sb("hq", [P, CC * BLK], BF16)
    wring = sb("wring", [P, 8192], BF16)
    qTw = sb("qTw", [P, max(HC * WIN, 4096)], BF16)
    xlT = sb("xlT", [P, HC * WIN], BF16)
    xtile = sb("xtile", [P, 2 * D], F32)
    hstr = sb("hstr", [P, 2 * 516], F32)
    scr_a = sb("scr_a", [P, D], F32)
    scr_b = sb("scr_b", [P, D], F32)
    scr_c = sb("scr_c", [P, D], F32)
    scr_d = sb("scr_d", [P, D], F32)
    u32 = sb("u32", [P, D], I32)
    Row = sb("Row", [P, 512], F32)
    wtile = sb("wtile", [P, 2 * D], F32)
    G = sb("G", [P, S], BF16)
    Gi = sb("Gi", [P, S], BF16)
    st = sb("st", [P, 64], F32)
    sxT = sb("sxT", [8, P], F32)
    idb = sb("idb_s", [P, P], BF16)
    idf = sb("idf_s", [P, P], F32)
    msk = sb("msk_s", [P, P], BF16)
    onf = sb("onf_s", [1, P], F32)
    on8 = sb("on8_s", [8, P], F32)
    cst = sb("cst_s", [P, 8], F32)
    onc = sb("onc_s", [P, 1], F32)
    bqk = sb("bqk_s", [P, 16], F32)
    cwt = sb("cw_s", [P, 3 * CC], F32)
    cbt = sb("cb_s", [P, CC], F32)
    tbf = sb("tbf", [P, D], BF16)

    pA = psf("pA", [P, 512])
    pB = psf("pB", [P, 512])
    pO = psf("pO", [P, 1024])
    pP = psf("pP", [P, 1024])
    pS = psf("pS", [P, 1024])

    sem_h = lambda n: ctx.enter_context(nc.semaphore(n))
    em = Em(xdv=xdv)
    for eng, nm in (("tensor", "s_te"), ("vector", "s_dv"),
                    ("scalar", "s_sc"), ("sync", "s_sy"),
                    ("gpsimd", "s_gq")):
        em.esem[eng] = em.sem(sem_h(nm), nm)
    em.set_pool("sync", [em.sem(sem_h(f"dqs{i}"), f"dqs{i}")
                         for i in range(8)])
    em.set_pool("gpsimd", [em.sem(sem_h(f"dqg{i}"), f"dqg{i}")
                           for i in range(2)])
    em.set_pool("scalar", [em.sem(sem_h(f"dqo{i}"), f"dqo{i}")
                           for i in range(4)])

    TE, DV, SC, SY, GP = "tensor", "vector", "scalar", "sync", "gpsimd"
    for r in ("pA", "pB", "pOl", "pOh", "pPl", "pPh", "pSA", "pSB"):
        em.psum_res.add(r)

    for ci, cv in enumerate((EPS, 0.0, 1.0)):
        nc.const_aps.register(F32, cv, cst[:, ci:ci + 1])

    tp_slot = [0]

    def pe_transpose(src_ap, dst_ap, reads, writes, f32mode=False, waits=(),
                     scale=None, scres=()):
        sl = tp_slot[0]
        tp_slot[0] ^= 1
        ps_ap = pS[:, sl * 512: sl * 512 + P]
        pres = "pSA" if sl == 0 else "pSB"
        ident = idf if f32mode else idb
        em.op(TE, lambda e, s=src_ap, p=ps_ap, i=ident: e.matmul(
            p, s, i[:, :], start=True, stop=True),
            reads=tuple(reads) + ("consts",), writes=(pres,), waits=waits)
        em.op(SC, lambda e, p=ps_ap, d=dst_ap, sc=scale: e.activation(
            d, p, ACT.Copy, bias=0.0, scale=(1.0 if sc is None else sc)),
            reads=(pres, "consts") + tuple(scres), writes=tuple(writes))

    def ln_stats(src_ap, src_res, psum_src=False, extra_reads=(),
                 junk=None, junk_res="scr_b"):
        rr = (src_res,) + tuple(extra_reads)
        rc_ = rr + ("consts",)
        jk = scr_b if junk is None else junk
        if psum_src:
            em.op(DV, lambda e, x=src_ap: e.reduce_sum(st[:, 0:1], x, AX),
                  reads=rr, writes=("st",))
            em.op(SC, lambda e, x=src_ap, j=jk: e.activation(
                j[:, 0:512], x[:, 0:512], ACT.Square, bias=0.0,
                scale=1.0), reads=rc_, writes=(junk_res,))
            em.op(SC, lambda e, x=src_ap, j=jk: e.activation(
                j[:, 512:1024], x[:, 512:1024], ACT.Square, bias=0.0,
                scale=1.0), reads=rc_, writes=(junk_res,))
            em.op(DV, lambda e, j=jk: e.reduce_sum(st[:, 1:2], j[:, :], AX),
                  reads=(junk_res,), writes=("st",))
        else:
            em.op(SC, lambda e, x=src_ap, j=jk: e.activation(
                j[:, :], x, ACT.Copy, bias=0.0, scale=1.0,
                accum_out=st[:, 0:1]),
                reads=rc_, writes=(junk_res, "st"))
            em.op(SC, lambda e, x=src_ap, j=jk: e.activation(
                j[:, :], x, ACT.Square, bias=0.0, scale=1.0,
                accum_out=st[:, 1:2]),
                reads=rc_, writes=(junk_res, "st"))
        em.op(DV, lambda e: e.tensor_scalar(st[:, 2:3], st[:, 0:1], 1.0 / D,
                                            None, ALU.mult),
              reads=("st",), writes=("st",))
        em.op(DV, lambda e: e.tensor_mul(st[:, 4:5], st[:, 2:3], st[:, 2:3]),
              reads=("st",), writes=("st",))
        em.op(DV, lambda e: e.scalar_tensor_tensor(
            st[:, 3:4], st[:, 1:2], 1.0 / D, st[:, 4:5],
            ALU.mult, ALU.subtract),
            reads=("st",), writes=("st",))
        em.op(SC, lambda e: e.activation(st[:, 5:6], st[:, 3:4], ACT.Sqrt,
                                         bias=EPS, scale=1.0),
              reads=("st", "consts"), writes=("st",))
        em.op(DV, lambda e: e.reciprocal(st[:, 6:7], st[:, 5:6]),
              reads=("st",), writes=("st",))

    for dst, src in ((idb, idb_h), (idf, idf_h), (msk, mk_h), (onf, onf_h),
                     (on8, on8_h), (onc, onc_h), (cst, cst_h),
                     (bqk, bqk_h), (cbt, cb_h)):
        em.dma(SY, dst[:, :], src[:, :], writes=("consts",))
    em.dma(SY, cwt[:, :], cw_h[:, :], writes=("cwt",))

    em.op(DV, lambda e: e.memset(G[:, 0:1], 1.0),
          reads=("consts",), writes=("G",))
    em.op(DV, lambda e: e.memset(G[:, 1:S], GAMMA),
          reads=("G",), writes=("G",))
    em.op(DV, lambda e: e.tensor_tensor_scan(
        G[:, 1:S], G[:, 1:S], G[:, 1:S], 1.0, ALU.mult, ALU.bypass),
        reads=("G",), writes=("G",))
    em.op(DV, lambda e: e.reciprocal(Gi[:, :], G[:, :]),
          reads=("G",), writes=("Gi",))

    for wi, wh in enumerate((wq_h, wk_h, wv_h)):
        em.dma(GP, wring[:, :], wh[:, :], writes=("wring0", "wring1"))
        for hc in range(HC):
            for oc in range(HC):
                src = wring[:, oc * D + hc * P: oc * D + (hc + 1) * P]
                dst = tbf[:, oc * P:(oc + 1) * P]
                pe_transpose(src, dst,
                             reads=("wring0", "wring1"), writes=("tbf",))
            em.dma(SY, scr_ap(aw_s, wi, HC * D, hc * D, D), tbf[:, :],
                   reads=("tbf",))

    def w_mean_pass(wh, ncols, col0):
        nchunk = ncols // D
        for ich in range(nchunk):
            xres = "wt%d" % (ich % 2)
            xsl = wtile[:, (ich % 2) * D:(ich % 2) * D + D]
            em.dma(GP, xsl, wh[:, ich * D:(ich + 1) * D], writes=(xres,))
            cdst = col0 + 1 + (ich % 8)
            if ich < 8:
                em.op(DV, lambda e, x=xsl, c=cdst: e.reduce_sum(
                    st[:, c:c + 1], x, AX, apply_absolute_value=True),
                    reads=(xres, "st"), writes=("st",))
            else:
                em.op(DV, lambda e, x=xsl: e.reduce_sum(
                    st[:, col0:col0 + 1], x, AX, apply_absolute_value=True),
                    reads=(xres, "st"), writes=("st",))
                em.op(DV, lambda e, c=cdst: e.tensor_add(
                    st[:, c:c + 1], st[:, c:c + 1], st[:, col0:col0 + 1]),
                    reads=("st",), writes=("st",))
        for stp in (4, 2, 1):
            for k in range(stp):
                em.op(DV, lambda e, a=col0 + 1 + k, b=col0 + 1 + k + stp:
                      e.tensor_add(st[:, a:a + 1], st[:, a:a + 1],
                                   st[:, b:b + 1]),
                      reads=("st",), writes=("st",))

    def w_scale_finish(col0, ps_col, sw_col, nmean_col):
        em.op(TE, lambda e: e.matmul(pS[0:1, ps_col:ps_col + 1],
                                     st[:, col0 + 1:col0 + 2], onc[:, :],
                                     start=True, stop=True),
              reads=("st", "consts"), writes=("pSB",))
        em.op(DV, lambda e: e.tensor_scalar(
            st[0:1, col0:col0 + 1], pS[0:1, ps_col:ps_col + 1],
            1.0 / (DH * D), 1e-5, ALU.mult, ALU.max),
            reads=("pSB",), writes=("st",))
        em.op(TE, lambda e: e.matmul(pS[:, ps_col + 1:ps_col + 2],
                                     onf[0:1, :], st[0:1, col0:col0 + 1],
                                     start=True, stop=True),
              reads=("st", "consts"), writes=("pSB",))
        em.op(DV, lambda e, c=nmean_col: e.tensor_scalar(
            st[:, c:c + 1], pS[:, ps_col + 1:ps_col + 2], -1.0, None,
            ALU.mult), reads=("pSB",), writes=("st",))
        em.op(DV, lambda e, c=nmean_col, o=sw_col: e.reciprocal(
            st[:, o:o + 1], st[:, c:c + 1]), reads=("st",), writes=("st",))
        em.op(DV, lambda e, o=sw_col: e.tensor_scalar(
            st[:, o:o + 1], st[:, o:o + 1], -1.0, None, ALU.mult),
            reads=("st",), writes=("st",))

    def tern_chunk(wh, ich, sw_col):
        xres = "wt%d" % (ich % 2)
        xsl = wtile[:, (ich % 2) * D:(ich % 2) * D + D]
        em.dma(GP, xsl, wh[:, ich * D:(ich + 1) * D], writes=(xres,))
        em.op(SC, lambda e, x=xsl, c=sw_col: e.activation(
            scr_c[:, :], x, ACT.Copy, bias=0.0, scale=st[:, c:c + 1]),
            reads=(xres, "st", "consts"), writes=("scr_c",))
        em.op(DV, lambda e: e.tensor_single_scalar(
            scr_d[:, :], scr_c[:, :], 0.5, ALU.is_ge),
            reads=("scr_c",), writes=("scr_d",))
        em.op(DV, lambda e: e.scalar_tensor_tensor(
            tbf[:, :], scr_c[:, :], -0.5, scr_d[:, :],
            ALU.is_lt, ALU.subtract),
            reads=("scr_c", "scr_d"), writes=("tbf",))

    w_mean_pass(w1_h, CC * D, 16)
    w_scale_finish(16, 896, 22, 23)
    em.op(DV, lambda e: e.tensor_scalar(
        cwt[:, :], cwt[:, :], st[:, 23:24], None, ALU.mult),
        reads=("cwt", "st"), writes=("cwt",))
    for cc in range(CC):
        tern_chunk(w1_h, cc, 22)
        for dc in range(HC):
            src = tbf[:, dc * P:(dc + 1) * P]
            dst = qTw[:, dc * 512 + (cc % 4) * P: dc * 512 + (cc % 4 + 1) * P]
            pe_transpose(src, dst,
                         reads=("tbf",), writes=("qTw",))
        if cc % 4 == 3:
            em.dma(SY, scr_ap(w1_s, cc // 4, HC * 512), qTw[:, 0:HC * 512],
                   reads=("qTw",))

    w_mean_pass(w2_h, HC * DH, 32)
    w_scale_finish(32, 900, 40, 41)
    for dc in range(HC):
        for g in range(DH // D):
            tern_chunk(w2_h, dc * 4 + g, 40)
            for k in range(8):
                cc = g * 8 + k
                src = tbf[:, k * P:(k + 1) * P]
                dst = qTw[:, cc * P:(cc + 1) * P]
                pe_transpose(src, dst,
                             reads=("tbf",), writes=("qTw",))
        em.dma(SY, scr_ap(w2_s, dc, CC * P), qTw[:, 0:CC * P],
               reads=("qTw",))

    for sq in range(NSEQ):
        em.region = "attn"
        for w in range(NW):
            w0 = w * WIN
            for tt in range(WT):
                ti = w0 // P + tt
                xres = "xt%d" % (tt % 2)
                xs = xtile[:, (tt % 2) * D:(tt % 2 + 1) * D]
                em.dma(SY, xs, x_h[sq, ti * P:(ti + 1) * P, :],
                       writes=(xres,))
                ln_stats(xs, xres)
                em.op(DV, lambda e, x=xs: e.tensor_scalar(
                    tbf[:, :], x, st[:, 2:3], st[:, 6:7],
                    ALU.subtract, ALU.mult),
                    reads=(xres, "st"), writes=("tbf",))
                for hc in range(HC):
                    src = tbf[:, hc * P:(hc + 1) * P]
                    dst = xlT[:, hc * WIN + tt * P: hc * WIN + (tt + 1) * P]
                    pe_transpose(src, dst,
                                 reads=("tbf",), writes=("xlT",))
            for wi in range(3):
                ld = em.dma(SY, wring[:, 0:HC * D], scr_ap(aw_s, wi, HC * D),
                            writes=("wring0", "wring1"))
                if wi < 2:
                    gtab = G if wi == 0 else Gi
                    gres = "G" if wi == 0 else "Gi"
                    boff = 0 if wi == 0 else 8
                    for oc in range(HC):
                        ps, pres = (pA, "pA") if oc % 2 == 0 else (pB, "pB")
                        ps_ap = ps[:, 0:WIN]
                        for hc in range(HC):
                            lhs = wring[:, hc * D + oc * P:
                                        hc * D + (oc + 1) * P]
                            rhs = xlT[:, hc * WIN:(hc + 1) * WIN]
                            em.op(TE, lambda e, o=ps_ap, l=lhs, r=rhs, h=hc:
                                  e.matmul(o, l, r, start=(h == 0),
                                           stop=(h == HC - 1)),
                                  reads=("wring0", "wring1", "xlT"),
                                  writes=(pres,), waits=[ld])
                        if wi == 0:
                            dst, dres = qTw[:, oc * WIN:(oc + 1) * WIN], "qTw"
                        else:
                            dst = kT[:, oc * S + w0: oc * S + w0 + WIN]
                            dres = "kT"
                        gsl = gtab[:, w0:w0 + WIN]
                        bcol = bqk[:, boff + oc: boff + oc + 1]
                        em.op(DV, lambda e, d=dst, p=ps_ap, g=gsl, b=bcol:
                              e.scalar_tensor_tensor(d, p, b, g, ALU.add,
                                                     ALU.mult),
                              reads=(pres, gres, "consts"), writes=(dres,))
                else:
                    for tt in range(WT):
                        ti = w0 // P + tt
                        for half in range(2):
                            ps, pres = (pA, "pA") if half == 0 else (pB, "pB")
                            ps_ap = ps[:, 0:512]
                            for hc in range(HC):
                                lhs = xlT[:, hc * WIN + tt * P:
                                          hc * WIN + (tt + 1) * P]
                                rhs = wring[:, hc * D + half * 512:
                                            hc * D + (half + 1) * 512]
                                em.op(TE, lambda e, o=ps_ap, l=lhs, r=rhs,
                                      h=hc: e.matmul(o, l, r,
                                                     start=(h == 0),
                                                     stop=(h == HC - 1)),
                                      reads=("wring0", "wring1", "xlT"),
                                      writes=(pres,), waits=[ld])
                            dst = vT[:, ti * D + half * 512:
                                     ti * D + (half + 1) * 512]
                            em.op(SC, lambda e, d=dst, p=ps_ap:
                                  e.activation(d, p, ACT.Copy, bias=0.0,
                                               scale=1.0),
                                  reads=(pres, "consts"), writes=("vT",))
            njt = w0 // P + WT
            for j in range(njt):
                ps, pres = (pA, "pA") if j % 2 == 0 else (pB, "pB")
                lo = max(0, j * P - w0)
                ncol = WIN - lo
                ps_ap = ps[:, 0:ncol]
                for hc in range(HC):
                    lhs = kT[:, hc * S + j * P: hc * S + (j + 1) * P]
                    rhs = qTw[:, hc * WIN + lo:(hc + 1) * WIN]
                    em.op(TE, lambda e, o=ps_ap, l=lhs, r=rhs, h=hc: e.matmul(
                        o, l, r, start=(h == 0), stop=(h == HC - 1)),
                        reads=("kT", "qTw"), writes=(pres,))
                sres = "hq%d" % j
                if j * P >= w0:
                    m_dst = hq[:, j * 512 + lo: j * 512 + lo + P]
                    em.op(DV, lambda e, d=m_dst, p=ps[:, 0:P]: e.tensor_mul(
                        d, p, msk[:, :]),
                        reads=(pres, "consts"), writes=(sres,))
                    if ncol > P:
                        r_dst = hq[:, j * 512 + lo + P: j * 512 + WIN]
                        em.op(SC, lambda e, d=r_dst, p=ps[:, P:ncol]:
                              e.activation(d, p, ACT.Copy, bias=0.0,
                                           scale=1.0),
                              reads=(pres, "consts"), writes=(sres,))
                else:
                    strip = hq[:, j * 512 + lo: j * 512 + WIN]
                    em.op(SC, lambda e, d=strip, p=ps_ap:
                          e.activation(d, p, ACT.Copy, bias=0.0, scale=1.0),
                          reads=(pres, "consts"), writes=(sres,))
            for tt in range(WT):
                i = w0 // P + tt
                xres = "xt%d" % (tt % 2)
                xs = xtile[:, (tt % 2) * D:(tt % 2 + 1) * D]
                em.dma(SY, xs, x_h[sq, i * P:(i + 1) * P, :], writes=(xres,))
                for j in range(i + 1):
                    lhs = hq[:, j * 512 + tt * P: j * 512 + (tt + 1) * P]
                    for half in range(2):
                        ps_ap = pO[:, half * 512:(half + 1) * 512]
                        pres = "pOl" if half == 0 else "pOh"
                        rhs = vT[:, j * D + half * 512:
                                 j * D + (half + 1) * 512]
                        em.op(TE, lambda e, o=ps_ap, l=lhs, r=rhs, jj=j,
                              ii=i: e.matmul(o, l, r, start=(jj == 0),
                                             stop=(jj == ii)),
                              reads=("hq%d" % j, "vT"), writes=(pres,))
                ln_stats(pO[:, :], "pOl", psum_src=True,
                         extra_reads=("pOh",))
                em.op(DV, lambda e: e.tensor_scalar(
                    scr_a[:, :], pO[:, :], st[:, 2:3], st[:, 6:7],
                    ALU.subtract, ALU.mult),
                    reads=("pOl", "pOh", "st"), writes=("scr_a",))
                dst = x2[:, i * D:(i + 1) * D]
                em.op(DV, lambda e, d=dst, x=xs: e.tensor_add(
                    d, scr_a[:, :], x),
                    reads=("scr_a", xres), writes=("x2",))
        em.region = "ffn"
        def emit_blk(b):
            em.region = "blk"
            t0 = b * BLK
            lo = t0 - 1
            clo, chi = max(lo, 0), min(t0 + BLK + 1, S)
            mid = t0 + BLK // 2
            lenA, lenB = mid - clo, chi - mid
            for cc in range(CC):
                ccg, sl4 = cc // 4, cc % 4
                if sl4 == 0:
                    w1ld = em.dma(SY, wring[:, 0:HC * 512],
                                  scr_ap(w1_s, ccg, HC * 512),
                                  writes=("wring0",))
                hres = "hs%d" % (cc % 2)
                hs = hstr[:, (cc % 2) * 516:(cc % 2) * 516 + 516]
                for (ps, pres, c0, ln) in ((pA, "pA", clo, lenA),
                                           (pB, "pB", mid, lenB)):
                    ps_ap = ps[:, 0:ln]
                    for dc in range(HC):
                        lhs = wring[:, dc * 512 + sl4 * P:
                                    dc * 512 + (sl4 + 1) * P]
                        rhs = kT[:, dc * S + c0: dc * S + c0 + ln]
                        em.op(TE, lambda e, o=ps_ap, l=lhs, r=rhs, h=dc:
                              e.matmul(o, l, r, start=(h == 0),
                                       stop=(h == HC - 1)),
                              reads=("wring0", "kT"), writes=(pres,),
                              waits=[w1ld])
                if clo > lo:
                    em.op(DV, lambda e, h=hs: e.memset(h[:, 0:1], 0.0),
                          reads=(), writes=(hres,))
                if chi < t0 + BLK + 1:
                    em.op(DV, lambda e, h=hs, c0_=chi - lo: e.memset(
                        h[:, c0_:BLK + 2], 0.0), reads=(), writes=(hres,))
                d_a = hs[:, clo - lo: clo - lo + lenA]
                d_b = hs[:, mid - lo: mid - lo + lenB]
                em.op(SC, lambda e, d=d_a, p=pA[:, 0:lenA]: e.activation(
                    d, p, ACT.Copy, bias=0.0, scale=1.0),
                    reads=("pA", "consts"), writes=(hres,))
                em.op(SC, lambda e, d=d_b, p=pB[:, 0:lenB]: e.activation(
                    d, p, ACT.Copy, bias=0.0, scale=1.0),
                    reads=("pB", "consts"), writes=(hres,))
                c0c = cwt[:, cc * 3 + 0: cc * 3 + 1]
                c1c = cwt[:, cc * 3 + 1: cc * 3 + 2]
                c2c = cwt[:, cc * 3 + 2: cc * 3 + 3]
                em.op(SC, lambda e, h=hs, c=c0c: e.activation(
                    scr_a[:, 0:BLK], h[:, 0:BLK], ACT.Copy, bias=0.0,
                    scale=c),
                    reads=(hres, "cwt", "consts"), writes=("scr_a",))
                em.op(DV, lambda e, h=hs, c=c1c: e.scalar_tensor_tensor(
                    scr_b[:, 0:BLK], h[:, 1:BLK + 1], c, scr_a[:, 0:BLK],
                    ALU.mult, ALU.add),
                    reads=(hres, "scr_a", "cwt"), writes=("scr_b",))
                em.op(DV, lambda e, h=hs, c=c2c: e.scalar_tensor_tensor(
                    scr_a[:, 0:BLK], h[:, 2:BLK + 2], c, scr_b[:, 0:BLK],
                    ALU.mult, ALU.add),
                    reads=(hres, "scr_b", "cwt"), writes=("scr_a",))
                dst = hq[:, cc * BLK:(cc + 1) * BLK]
                em.op(SC, lambda e, d=dst, c=cbt[:, cc:cc + 1]: e.activation(
                    d, scr_a[:, 0:BLK], ACT.Silu, bias=c, scale=1.0),
                    reads=("scr_a", "consts"), writes=("hq%d" % cc,))
                if cc % 8 == 7:
                    g = cc // 8
                    grp = tuple("hq%d" % c for c in range(g * 8, cc + 1))
                    rdst = scr_d if g == 0 else scr_c
                    em.op(DV, lambda e, d=rdst, g0=g * 8: e.reduce_max(
                        d[:, 0:BLK, None],
                        hq[:, g0 * BLK:(g0 + 8) * BLK].rearrange(
                            "p (c t) -> p t c", c=8), AX,
                        apply_absolute_value=True),
                        reads=grp, writes=("scr_d" if g == 0 else "scr_c",))
                    if g > 0:
                        em.op(DV, lambda e: e.tensor_max(
                            scr_d[:, 0:BLK], scr_d[:, 0:BLK],
                            scr_c[:, 0:BLK]),
                            reads=("scr_c", "scr_d"), writes=("scr_d",))
            for q4 in range(BT):
                src = scr_d[:, q4 * P:(q4 + 1) * P]
                sl = tp_slot[0]
                tp_slot[0] ^= 1
                ps_ap = pS[:, sl * 512: sl * 512 + P]
                pres = "pSA" if sl == 0 else "pSB"
                em.op(TE, lambda e, s=src, p=ps_ap: e.matmul(
                    p, s, idf[:, :], start=True, stop=True),
                    reads=("scr_d", "consts"), writes=(pres,))
                em.op(DV, lambda e, p=ps_ap, q=q4: e.reduce_max(
                    st[:, 10 + q:11 + q], p, AX),
                    reads=(pres,), writes=("st",))
            em.op(DV, lambda e: e.tensor_single_scalar(
                st[:, 10:10 + BT], st[:, 10:10 + BT], 1e-5, ALU.max),
                reads=("st",), writes=("st",))
            em.op(DV, lambda e: e.reciprocal(st[:, 14:14 + BT],
                                             st[:, 10:10 + BT]),
                  reads=("st",), writes=("st",))
            em.op(DV, lambda e: e.tensor_single_scalar(
                st[:, 14:14 + BT], st[:, 14:14 + BT], 127.0, ALU.mult),
                reads=("st",), writes=("st",))
            em.op(DV, lambda e: e.reciprocal(st[:, 18:18 + BT],
                                             st[:, 14:14 + BT]),
                  reads=("st",), writes=("st",))
            em.op(DV, lambda e: e.tensor_scalar(
                st[:, 44:44 + BT], st[:, 18:18 + BT], st[:, 41:42],
                None, ALU.mult), reads=("st",), writes=("st",))
            sl = tp_slot[0]
            tp_slot[0] ^= 1
            ps_ap = pS[0:BT, sl * 512: sl * 512 + P]
            pres = "pSA" if sl == 0 else "pSB"
            em.op(TE, lambda e, p=ps_ap: e.matmul(
                p, st[:, 14:14 + BT], idf[:, :], start=True, stop=True),
                reads=("st", "consts"), writes=(pres,))
            em.op(DV, lambda e, p=ps_ap: e.tensor_copy(sxT[0:BT, :], p),
                  reads=(pres,), writes=("sxT",))
            for q4 in range(BT):
                ps_r = pS[:, 768:768 + P]
                em.op(DV, lambda e, r=q4: e.tensor_scalar(
                    scr_b[0:BT, 0:P], sxT[0:BT, :],
                    idf[0:BT, r:r + 1], None, ALU.mult),
                    reads=("sxT", "consts"), writes=("scr_b",))
                em.op(TE, lambda e, p=ps_r: e.matmul(
                    p, on8[0:BT, :], scr_b[0:BT, 0:P],
                    start=True, stop=True),
                    reads=("scr_b", "consts"), writes=("pSB",))
                em.op(SC, lambda e, p=ps_r, q=q4:
                      e.activation(Row[:, q * P:(q + 1) * P], p, ACT.Copy,
                                   bias=0.0, scale=1.0),
                      reads=("pSB", "consts"), writes=("Row",))
            for cc in range(CC):
                hres = "hq%d" % cc
                ssl = hq[:, cc * BLK:(cc + 1) * BLK]
                em.op(DV, lambda e, s=ssl: e.tensor_mul(
                    scr_a[:, 0:BLK], s, Row[:, 0:BLK]),
                    reads=(hres, "Row"), writes=("scr_a",))
                em.op(DV, lambda e: e.tensor_scalar(
                    u32[:, 0:BLK], scr_a[:, 0:BLK], 0.5, None, ALU.add),
                    reads=("scr_a",), writes=("u32",))
                em.op(DV, lambda e: e.tensor_copy(scr_c[:, 0:BLK],
                                                  u32[:, 0:BLK]),
                      reads=("u32",), writes=("scr_c",))
                em.op(DV, lambda e: e.scalar_tensor_tensor(
                    scr_b[:, 0:BLK], scr_a[:, 0:BLK], 0.5, scr_c[:, 0:BLK],
                    ALU.add, ALU.is_lt),
                    reads=("scr_a", "scr_c"), writes=("scr_b",))
                em.op(DV, lambda e, s=ssl: e.scalar_tensor_tensor(
                    s, scr_b[:, 0:BLK], -1.0, scr_c[:, 0:BLK],
                    ALU.mult, ALU.add),
                    reads=("scr_b", "scr_c"), writes=(hres,))
            psums = [(pA[:, 0:BLK], "pA"), (pB[:, 0:BLK], "pB"),
                     (pO[:, 0:BLK], "pOl"), (pO[:, 512:512 + BLK], "pOh"),
                     (pP[:, 0:BLK], "pPl"), (pP[:, 512:512 + BLK], "pPh"),
                     (pS[:, 0:BLK], "pSA"), (pS[:, 512:512 + BLK], "pSB")]
            for dc in range(HC):
                w2ld = em.dma(SY, wring[:, 4096:4096 + CC * P],
                              scr_ap(w2_s, dc, CC * P), writes=("wring1",))
                ps_ap, pres = psums[dc]
                for cc in range(CC):
                    lhs = wring[:, 4096 + cc * P:4096 + (cc + 1) * P]
                    rhs = hq[:, cc * BLK:(cc + 1) * BLK]
                    em.op(TE, lambda e, o=ps_ap, l=lhs, r=rhs, c=cc:
                          e.matmul(o, l, r, start=(c == 0),
                                   stop=(c == CC - 1)),
                          reads=("wring1", "hq%d" % cc), writes=(pres,),
                          waits=[w2ld])
            for dc in range(HC):
                em.op(SC, lambda e, d=hq[:, dc * BLK:(dc + 1) * BLK],
                      p=psums[dc][0]:
                      e.activation(d, p, ACT.Copy, bias=0.0, scale=1.0),
                      reads=(psums[dc][1], "consts"),
                      writes=("y2s", "hq%d" % dc))
            for q4 in range(BT):
                i = t0 // P + q4
                for dc in range(HC):
                    src = hq[:, dc * BLK + q4 * P: dc * BLK + (q4 + 1) * P]
                    dst = scr_d[:, dc * P:(dc + 1) * P]
                    pe_transpose(src, dst, reads=("y2s",), writes=("scr_d",),
                                 scale=st[:, 44 + q4:45 + q4], scres=("st",))
                ln_stats(scr_d[:, :], "scr_d", junk=scr_a, junk_res="scr_a")
                em.op(DV, lambda e: e.tensor_scalar(
                    scr_a[:, :], scr_d[:, :], st[:, 2:3], st[:, 6:7],
                    ALU.subtract, ALU.mult),
                    reads=("scr_d", "st"), writes=("scr_a",))
                em.op(DV, lambda e, x=x2[:, i * D:(i + 1) * D]:
                      e.tensor_add(scr_b[:, :], scr_a[:, :], x),
                      reads=("scr_a", "x2"), writes=("scr_b",))
                em.dma(SC, y_h[sq, i * P:(i + 1) * P, :], scr_b[:, :],
                       reads=("scr_b",))
        for i in range(NT):
            xsl = x2[:, i * D:(i + 1) * D]
            ln_stats(xsl, "x2", junk=scr_a, junk_res="scr_a")
            em.op(DV, lambda e, x=xsl: e.tensor_scalar(
                scr_a[:, :], x, st[:, 2:3], st[:, 6:7],
                ALU.subtract, ALU.mult),
                reads=("x2", "st"), writes=("scr_a",))
            em.op(DV, lambda e: e.reduce_max(st[:, 7:8], scr_a[:, :], AX,
                                             apply_absolute_value=True),
                  reads=("scr_a",), writes=("st",))
            em.op(DV, lambda e: e.tensor_single_scalar(
                st[:, 7:8], st[:, 7:8], 1e-5, ALU.max),
                reads=("st",), writes=("st",))
            em.op(DV, lambda e: e.reciprocal(st[:, 8:9], st[:, 7:8]),
                  reads=("st",), writes=("st",))
            em.op(DV, lambda e: e.tensor_single_scalar(
                st[:, 8:9], st[:, 8:9], 127.0, ALU.mult),
                reads=("st",), writes=("st",))
            em.op(DV, lambda e: e.reciprocal(st[:, 9:10], st[:, 8:9]),
                  reads=("st",), writes=("st",))
            em.op(SC, lambda e: e.activation(
                scr_b[:, :], scr_a[:, :], ACT.Copy, bias=0.0,
                scale=st[:, 8:9]),
                reads=("scr_a", "st", "consts"), writes=("scr_b",))
            em.op(DV, lambda e: e.tensor_scalar(
                u32[:, :], scr_b[:, :], 0.5, None, ALU.add),
                reads=("scr_b",), writes=("u32",))
            em.op(DV, lambda e: e.tensor_copy(scr_c[:, :], u32[:, :]),
                  reads=("u32",), writes=("scr_c",))
            em.op(DV, lambda e: e.scalar_tensor_tensor(
                scr_d[:, :], scr_b[:, :], 0.5, scr_c[:, :],
                ALU.add, ALU.is_lt),
                reads=("scr_b", "scr_c"), writes=("scr_d",))
            em.op(DV, lambda e: e.scalar_tensor_tensor(
                scr_b[:, :], scr_d[:, :], -1.0, scr_c[:, :],
                ALU.mult, ALU.add),
                reads=("scr_d", "scr_c"), writes=("scr_b",))
            em.op(DV, lambda e: e.tensor_scalar(
                tbf[:, :], scr_b[:, :], st[:, 9:10], None, ALU.mult),
                reads=("scr_b", "st"), writes=("tbf",))
            for hc in range(HC):
                src = tbf[:, hc * P:(hc + 1) * P]
                dst = kT[:, hc * S + i * P: hc * S + (i + 1) * P]
                pe_transpose(src, dst,
                             reads=("tbf",), writes=("kT",))
            for b_ in range(NBLK):
                if (i == (b_ + 1) * BT) if b_ < NBLK - 1 else (i == NT - 1):
                    emit_blk(b_)

    em._raw_op(SY, None, em.dma_all(SC) + em.dma_all(SY) + em.dma_all(GP),
               ())

    block = ctx.enter_context(nc.Block())
    with nc.allow_low_precision(reason="bf16 matmul staging; fp32 accum"):
        em.emit(block)
    ctx.close()
    return nc


def _host_pack(Wq, bq, Wk, bk, Wv, w1, conv_w, conv_b, w2):
    f32 = np.float32
    pk = {}
    pk["wq"] = np.ascontiguousarray(
        np.asarray(Wq, f32).reshape(HC, P, D).transpose(1, 0, 2)
        .reshape(P, HC * D))
    pk["wk"] = np.ascontiguousarray(
        np.asarray(Wk, f32).reshape(HC, P, D).transpose(1, 0, 2)
        .reshape(P, HC * D))
    pk["wv"] = np.ascontiguousarray(
        np.asarray(Wv, f32).reshape(HC, P, D).transpose(1, 0, 2)
        .reshape(P, HC * D))
    bqk = np.zeros((P, 16), f32)
    bqk[:, 0:8] = np.asarray(bq, f32).reshape(HC, P).T
    bqk[:, 8:16] = np.asarray(bk, f32).reshape(HC, P).T
    pk["bqk"] = bqk
    pk["w1"] = np.ascontiguousarray(
        np.asarray(w1, f32).reshape(CC, P, D).transpose(1, 0, 2)
        .reshape(P, CC * D))
    pk["w2"] = np.ascontiguousarray(
        np.asarray(w2, f32).reshape(HC, P, DH).transpose(1, 0, 2)
        .reshape(P, HC * DH))
    pk["cw"] = np.ascontiguousarray(
        np.asarray(conv_w, f32).reshape(CC, P, 3).transpose(1, 0, 2)
        .reshape(P, CC * 3))
    pk["cb"] = np.ascontiguousarray(np.asarray(conv_b, f32).reshape(CC, P).T)
    pk["idb"] = np.eye(P, dtype=ml_dtypes.bfloat16)
    pk["idf"] = np.eye(P, dtype=np.float32)
    pk["msk"] = (np.arange(P)[None, :] >= np.arange(P)[:, None]).astype(
        ml_dtypes.bfloat16)
    pk["onf"] = np.ones((1, P), np.float32)
    pk["on8"] = np.ones((8, P), np.float32)
    pk["onc"] = np.ones((P, 1), np.float32)
    pk["cst"] = np.tile(np.array([[EPS, 0.0, 1.0, 0, 0, 0, 0, 0]],
                                 np.float32), (P, 1))
    return pk


def kernel(L0_attn_x, L0_attn_Wq, L0_attn_bq, L0_attn_Wk, L0_attn_bk,
           L0_attn_Wv, L0_ffn_w1, L0_ffn_conv_w, L0_ffn_conv_b, L0_ffn_w2):
    B, S, _ = L0_attn_x.shape
    NSEQ = B // NCORES
    pk = _host_pack(L0_attn_Wq, L0_attn_bq, L0_attn_Wk, L0_attn_bk,
                    L0_attn_Wv, L0_ffn_w1, L0_ffn_conv_w, L0_ffn_conv_b,
                    L0_ffn_w2)
    nc = build_nc(NSEQ, S, xdv=())
    x = np.ascontiguousarray(np.asarray(L0_attn_x, np.float32))
    in_maps = []
    for c in range(NCORES):
        m = dict(pk)
        m["x"] = x[c * NSEQ:(c + 1) * NSEQ]
        in_maps.append(m)
    res = run_bass_kernel_spmd(nc, in_maps, core_ids=list(range(NCORES)))
    out = np.concatenate([np.asarray(r["y"]) for r in res.results], axis=0)
    return out.astype(np.float32, copy=False)


# version 11
# speedup vs baseline: 1.0050x; 1.0050x over previous
import numpy as np
import ml_dtypes
import concourse.bass as bass
import concourse.mybir as mybir
from concourse.bass_utils import run_bass_kernel_spmd

F32 = mybir.dt.float32
BF16 = mybir.dt.bfloat16
I32 = mybir.dt.int32
AX = mybir.AxisListType.X
ALU = mybir.AluOpType
ACT = mybir.ActivationFunctionType

P = 128
D = 1024
DH = 4096
HC = D // P
CC = DH // P
GAMMA = 0.96875
EPS = 1e-5
NCORES = 8


class Em:

    ENGINES = ("sync", "gpsimd", "tensor", "vector", "scalar")

    def __init__(self, xdv=()):
        self.xdv = set(xdv)
        self.region = "pro"
        self.q = {e: [] for e in self.ENGINES}
        self.cnt = {}
        self.esem = {}
        self.wr = {}
        self.rd = {}
        self.psum_res = set()

    def _xtra(self, eng=None):
        glue = "glue" in self.xdv and eng != "tensor"
        ls = ("ls_" + self.region) in self.xdv or "ls" in self.xdv
        out = ()
        if glue or ls:
            out = tuple((sm, self.cnt[sm[1]]) for sm in self.esem.values())
            for pl in self.pool.values():
                out = out + tuple((sm, self.cnt[sm[1]]) for sm in pl)
        elif eng == "tensor":
            if "gt_d" in self.xdv:
                for pl in self.pool.values():
                    out = out + tuple((sm, self.cnt[sm[1]]) for sm in pl)
            if "gt_e" in self.xdv:
                out = out + tuple((sm, self.cnt[sm[1]])
                                  for sm in self.esem.values())
        return out

    def sem(self, handle, name):
        self.cnt[name] = 0
        return (handle, name)

    def val(self, sem):
        return self.cnt[sem[1]]

    def _raw_op(self, eng, fn, waits, incs):
        for (s, v) in waits:
            assert v <= self.cnt[s[1]], \
                f"forward wait {s[1]}>={v} emitted={self.cnt[s[1]]}"
        for (s, n) in incs:
            self.cnt[s[1]] += n
        self.q[eng].append((fn, tuple(waits), tuple(incs)))

    def _collect(self, own, reads, writes, waits):
        w = {}

        def add(tok):
            if tok is None:
                return
            s, v = tok
            if own is not None and s[1] == own[1]:
                return
            if s[1] in w:
                w[s[1]] = (s, max(v, w[s[1]][1]))
            else:
                w[s[1]] = (s, v)

        for r in reads:
            add(self.wr.get(r))
            if r in self.psum_res:
                for tok in self.rd.get(r, {}).values():
                    add(tok)
        for r in writes:
            add(self.wr.get(r))
            for tok in self.rd.get(r, {}).values():
                add(tok)
        for tok in waits:
            add(tok)
        return list(w.values())

    def _post(self, tok, reads, writes, key):
        for r in writes:
            self.wr[r] = tok
            self.rd[r] = {}
        for r in reads:
            self.rd.setdefault(r, {})[key] = tok

    def op(self, eng, fn, reads=(), writes=(), waits=()):
        es = self.esem[eng]
        waits = tuple(waits) + self._xtra(eng)
        w = self._collect(None, reads, writes, waits)
        self._raw_op(eng, fn, w, [(es, 1)])
        tok = (es, self.cnt[es[1]])
        self._post(tok, reads, writes, es[1])
        return tok

    def set_pool(self, eng, sems):
        if not hasattr(self, "pool"):
            self.pool = {}
            self.pool_n = {}
        self.pool[eng] = sems
        self.pool_n[eng] = 0

    def dma(self, eng, out_ap, in_ap, reads=(), writes=(), waits=()):
        pool = self.pool[eng]
        n = self.pool_n[eng]
        sm = pool[n % len(pool)]
        prev = 16 * (n // len(pool))
        waits = tuple(waits) + self._xtra(eng)
        w = self._collect(None, reads, writes,
                          tuple(waits) + ((sm, prev),))
        self._raw_op(eng, lambda e, o=out_ap, i=in_ap:
                     e.dma_start(out=o, in_=i), w, [(sm, 16)])
        self.pool_n[eng] = n + 1
        tok = (sm, prev + 16)
        self._post(tok, reads, writes, sm[1])
        return tok

    def dma_all(self, eng):
        return [(sm, self.cnt[sm[1]]) for sm in self.pool[eng]]

    def emit(self, block):
        def body(eng_obj, name):
            hwm = {}
            for (fn, waits, incs) in self.q[name]:
                for (s, v) in waits:
                    if v > 0 and v > hwm.get(s[1], 0):
                        eng_obj.wait_ge(s[0], v)
                        hwm[s[1]] = v
                ins = fn(eng_obj) if fn is not None else None
                for (s, n) in incs:
                    if ins is None:
                        eng_obj.sem_inc(s[0], n)
                    else:
                        ins.then_inc(s[0], n)
        block.sync(lambda e: body(e, "sync"))
        block.gpsimd(lambda e: body(e, "gpsimd"))
        block.tensor(lambda e: body(e, "tensor"))
        block.vector(lambda e: body(e, "vector"))
        block.scalar(lambda e: body(e, "scalar"))


def build_nc(NSEQ, S, xdv=()):
    NT = S // P
    NW = max(1, S // 512)
    WIN = S // NW
    WT = WIN // P
    NBLK = max(1, S // 512)
    BLK = S // NBLK
    BT = BLK // P

    nc = bass.Bass("TRN2", target_bir_lowering=False,
                   detect_race_conditions=False,
                   dynamic_dma_scratch_size=8192)

    x_h = nc.dram_tensor("x", [NSEQ, S, D], F32, kind="ExternalInput")
    y_h = nc.dram_tensor("y", [NSEQ, S, D], F32, kind="ExternalOutput")
    wq_h = nc.dram_tensor("wq", [P, HC * D], F32, kind="ExternalInput")
    wk_h = nc.dram_tensor("wk", [P, HC * D], F32, kind="ExternalInput")
    wv_h = nc.dram_tensor("wv", [P, HC * D], F32, kind="ExternalInput")
    bqk_h = nc.dram_tensor("bqk", [P, 16], F32, kind="ExternalInput")
    w1_h = nc.dram_tensor("w1", [P, CC * D], F32, kind="ExternalInput")
    w2_h = nc.dram_tensor("w2", [P, HC * DH], F32, kind="ExternalInput")
    cw_h = nc.dram_tensor("cw", [P, 3 * CC], F32, kind="ExternalInput")
    cb_h = nc.dram_tensor("cb", [P, CC], F32, kind="ExternalInput")
    idb_h = nc.dram_tensor("idb", [P, P], BF16, kind="ExternalInput")
    idf_h = nc.dram_tensor("idf", [P, P], F32, kind="ExternalInput")
    mk_h = nc.dram_tensor("msk", [P, P], BF16, kind="ExternalInput")
    onf_h = nc.dram_tensor("onf", [1, P], F32, kind="ExternalInput")
    on8_h = nc.dram_tensor("on8", [8, P], F32, kind="ExternalInput")
    onc_h = nc.dram_tensor("onc", [P, 1], F32, kind="ExternalInput")
    cst_h = nc.dram_tensor("cst", [P, 8], F32, kind="ExternalInput")

    aw_s = nc.dram_tensor("aw_s", [3, P, HC * D], BF16, kind="Internal")
    w1_s = nc.dram_tensor("w1_s", [CC // 4, P, HC * 512], BF16,
                          kind="Internal")
    w2_s = nc.dram_tensor("w2_s", [HC, P, CC * P], BF16, kind="Internal")

    def scr_ap(t, row, ncols, coff=0, ccnt=None):
        return bass.AP(t, row * P * ncols + coff,
                       [[ncols, P], [1, ccnt if ccnt is not None else ncols]])

    from contextlib import ExitStack
    ctx = ExitStack()
    sb = lambda n, s, d: ctx.enter_context(nc.sbuf_tensor(n, s, d))
    psf = lambda n, s: ctx.enter_context(nc.psum_tensor(n, s, F32))

    kT = sb("kT", [P, HC * S], BF16)
    vT = sb("vT", [P, NT * D], BF16)
    x2 = sb("x2", [P, NT * D], BF16)
    hq = sb("hq", [P, CC * BLK], BF16)
    wring = sb("wring", [P, 8192], BF16)
    qTw = sb("qTw", [P, max(HC * WIN, 4096)], BF16)
    xlT = sb("xlT", [P, HC * WIN], BF16)
    xtile = sb("xtile", [P, 2 * D], F32)
    hstr = sb("hstr", [P, 2 * 516], F32)
    scr_a = sb("scr_a", [P, D], F32)
    scr_b = sb("scr_b", [P, D], F32)
    scr_c = sb("scr_c", [P, D], F32)
    scr_d = sb("scr_d", [P, D], F32)
    scjunk = sb("scjunk", [P, D], F32)
    u32 = sb("u32", [P, D], I32)
    M_s = sb("M_s", [P, 512], BF16)
    Row = sb("Row", [P, 512], F32)
    Rowo = sb("Rowo", [P, 512], F32)
    G = sb("G", [P, S], BF16)
    Gi = sb("Gi", [P, S], BF16)
    st = sb("st", [P, 64], F32)
    sxT = sb("sxT", [8, P], F32)
    idb = sb("idb_s", [P, P], BF16)
    idf = sb("idf_s", [P, P], F32)
    msk = sb("msk_s", [P, P], BF16)
    onf = sb("onf_s", [1, P], F32)
    on8 = sb("on8_s", [8, P], F32)
    cst = sb("cst_s", [P, 8], F32)
    sxm = sb("sxm_s", [8, P], F32)
    onc = sb("onc_s", [P, 1], F32)
    bqk = sb("bqk_s", [P, 16], F32)
    cwt = sb("cw_s", [P, 3 * CC], F32)
    cbt = sb("cb_s", [P, CC], F32)
    tbf = sb("tbf", [P, D], BF16)

    pA = psf("pA", [P, 512])
    pB = psf("pB", [P, 512])
    pO = psf("pO", [P, 1024])
    pP = psf("pP", [P, 1024])
    pS = psf("pS", [P, 1024])

    sem_h = lambda n: ctx.enter_context(nc.semaphore(n))
    em = Em(xdv=xdv)
    for eng, nm in (("tensor", "s_te"), ("vector", "s_dv"),
                    ("scalar", "s_sc"), ("sync", "s_sy"),
                    ("gpsimd", "s_gq")):
        em.esem[eng] = em.sem(sem_h(nm), nm)
    em.set_pool("sync", [em.sem(sem_h(f"dqs{i}"), f"dqs{i}")
                         for i in range(8)])
    em.set_pool("gpsimd", [em.sem(sem_h(f"dqg{i}"), f"dqg{i}")
                           for i in range(2)])
    em.set_pool("scalar", [em.sem(sem_h(f"dqo{i}"), f"dqo{i}")
                           for i in range(4)])

    TE, DV, SC, SY, GP = "tensor", "vector", "scalar", "sync", "gpsimd"
    for r in ("pA", "pB", "pOl", "pOh", "pPl", "pPh", "pSA", "pSB"):
        em.psum_res.add(r)

    for ci, cv in enumerate((EPS, 0.0, 1.0)):
        nc.const_aps.register(F32, cv, cst[:, ci:ci + 1])

    tp_slot = [0]

    def pe_transpose(src_ap, dst_ap, reads, writes, f32mode=False, waits=(),
                     scale=None, scres=()):
        sl = tp_slot[0]
        tp_slot[0] ^= 1
        ps_ap = pS[:, sl * 512: sl * 512 + P]
        pres = "pSA" if sl == 0 else "pSB"
        ident = idf if f32mode else idb
        em.op(TE, lambda e, s=src_ap, p=ps_ap, i=ident: e.matmul(
            p, s, i[:, :], start=True, stop=True),
            reads=tuple(reads) + ("consts",), writes=(pres,), waits=waits)
        em.op(SC, lambda e, p=ps_ap, d=dst_ap, sc=scale: e.activation(
            d, p, ACT.Copy, bias=0.0, scale=(1.0 if sc is None else sc)),
            reads=(pres, "consts") + tuple(scres), writes=tuple(writes))

    def ln_stats(src_ap, src_res, psum_src=False, extra_reads=()):
        rr = (src_res,) + tuple(extra_reads)
        rc_ = rr + ("consts",)
        if psum_src:
            em.op(DV, lambda e, x=src_ap: e.reduce_sum(st[:, 0:1], x, AX),
                  reads=rr, writes=("st",))
            em.op(SC, lambda e, x=src_ap: e.activation(
                scjunk[:, 0:512], x[:, 0:512], ACT.Square, bias=0.0,
                scale=1.0), reads=rc_, writes=("scjunk",))
            em.op(SC, lambda e, x=src_ap: e.activation(
                scjunk[:, 512:1024], x[:, 512:1024], ACT.Square, bias=0.0,
                scale=1.0), reads=rc_, writes=("scjunk",))
            em.op(DV, lambda e: e.reduce_sum(st[:, 1:2], scjunk[:, :], AX),
                  reads=("scjunk",), writes=("st",))
        else:
            em.op(SC, lambda e, x=src_ap: e.activation(
                scjunk[:, :], x, ACT.Copy, bias=0.0, scale=1.0,
                accum_out=st[:, 0:1]),
                reads=rc_, writes=("scjunk", "st"))
            em.op(SC, lambda e, x=src_ap: e.activation(
                scjunk[:, :], x, ACT.Square, bias=0.0, scale=1.0,
                accum_out=st[:, 1:2]),
                reads=rc_, writes=("scjunk", "st"))
        em.op(DV, lambda e: e.tensor_scalar(st[:, 2:3], st[:, 0:1], 1.0 / D,
                                            None, ALU.mult),
              reads=("st",), writes=("st",))
        em.op(DV, lambda e: e.tensor_mul(st[:, 4:5], st[:, 2:3], st[:, 2:3]),
              reads=("st",), writes=("st",))
        em.op(DV, lambda e: e.scalar_tensor_tensor(
            st[:, 3:4], st[:, 1:2], 1.0 / D, st[:, 4:5],
            ALU.mult, ALU.subtract),
            reads=("st",), writes=("st",))
        em.op(SC, lambda e: e.activation(st[:, 5:6], st[:, 3:4], ACT.Sqrt,
                                         bias=EPS, scale=1.0),
              reads=("st", "consts"), writes=("st",))
        em.op(DV, lambda e: e.reciprocal(st[:, 6:7], st[:, 5:6]),
              reads=("st",), writes=("st",))

    for dst, src in ((idb, idb_h), (idf, idf_h), (msk, mk_h), (onf, onf_h),
                     (on8, on8_h), (onc, onc_h), (cst, cst_h),
                     (bqk, bqk_h), (cbt, cb_h)):
        em.dma(SY, dst[:, :], src[:, :], writes=("consts",))
    em.dma(SY, cwt[:, :], cw_h[:, :], writes=("cwt",))

    em.op(DV, lambda e: e.memset(G[:, 0:1], 1.0),
          reads=("consts",), writes=("G",))
    em.op(DV, lambda e: e.memset(G[:, 1:S], GAMMA),
          reads=("G",), writes=("G",))
    em.op(DV, lambda e: e.tensor_tensor_scan(
        G[:, 1:S], G[:, 1:S], G[:, 1:S], 1.0, ALU.mult, ALU.bypass),
        reads=("G",), writes=("G",))
    em.op(DV, lambda e: e.reciprocal(Gi[:, :], G[:, :]),
          reads=("G",), writes=("Gi",))

    for wi, wh in enumerate((wq_h, wk_h, wv_h)):
        em.dma(GP, wring[:, :], wh[:, :], writes=("wring0", "wring1"))
        for hc in range(HC):
            for oc in range(HC):
                src = wring[:, oc * D + hc * P: oc * D + (hc + 1) * P]
                dst = tbf[:, oc * P:(oc + 1) * P]
                pe_transpose(src, dst,
                             reads=("wring0", "wring1"), writes=("tbf",))
            em.dma(SY, scr_ap(aw_s, wi, HC * D, hc * D, D), tbf[:, :],
                   reads=("tbf",))

    def w_mean_pass(wh, ncols, col0):
        nchunk = ncols // D
        for ich in range(nchunk):
            xres = "xt%d" % (ich % 2)
            xsl = xtile[:, (ich % 2) * D:(ich % 2) * D + D]
            em.dma(SY, xsl, wh[:, ich * D:(ich + 1) * D], writes=(xres,))
            cdst = col0 + 1 + (ich % 8)
            if ich < 8:
                em.op(DV, lambda e, x=xsl, c=cdst: e.reduce_sum(
                    st[:, c:c + 1], x, AX, apply_absolute_value=True),
                    reads=(xres, "st"), writes=("st",))
            else:
                em.op(DV, lambda e, x=xsl: e.reduce_sum(
                    st[:, col0:col0 + 1], x, AX, apply_absolute_value=True),
                    reads=(xres, "st"), writes=("st",))
                em.op(DV, lambda e, c=cdst: e.tensor_add(
                    st[:, c:c + 1], st[:, c:c + 1], st[:, col0:col0 + 1]),
                    reads=("st",), writes=("st",))
        for stp in (4, 2, 1):
            for k in range(stp):
                em.op(DV, lambda e, a=col0 + 1 + k, b=col0 + 1 + k + stp:
                      e.tensor_add(st[:, a:a + 1], st[:, a:a + 1],
                                   st[:, b:b + 1]),
                      reads=("st",), writes=("st",))

    def w_scale_finish(col0, ps_col, sw_col, nmean_col):
        em.op(TE, lambda e: e.matmul(pS[0:1, ps_col:ps_col + 1],
                                     st[:, col0 + 1:col0 + 2], onc[:, :],
                                     start=True, stop=True),
              reads=("st", "consts"), writes=("pSB",))
        em.op(DV, lambda e: e.tensor_scalar(
            st[0:1, col0:col0 + 1], pS[0:1, ps_col:ps_col + 1],
            1.0 / (DH * D), 1e-5, ALU.mult, ALU.max),
            reads=("pSB",), writes=("st",))
        em.op(TE, lambda e: e.matmul(pS[:, ps_col + 1:ps_col + 2],
                                     onf[0:1, :], st[0:1, col0:col0 + 1],
                                     start=True, stop=True),
              reads=("st", "consts"), writes=("pSB",))
        em.op(DV, lambda e, c=nmean_col: e.tensor_scalar(
            st[:, c:c + 1], pS[:, ps_col + 1:ps_col + 2], -1.0, None,
            ALU.mult), reads=("pSB",), writes=("st",))
        em.op(DV, lambda e, c=nmean_col, o=sw_col: e.reciprocal(
            st[:, o:o + 1], st[:, c:c + 1]), reads=("st",), writes=("st",))
        em.op(DV, lambda e, o=sw_col: e.tensor_scalar(
            st[:, o:o + 1], st[:, o:o + 1], -1.0, None, ALU.mult),
            reads=("st",), writes=("st",))

    def tern_chunk(wh, ich, sw_col):
        xres = "xt%d" % (ich % 2)
        xsl = xtile[:, (ich % 2) * D:(ich % 2) * D + D]
        em.dma(SY, xsl, wh[:, ich * D:(ich + 1) * D], writes=(xres,))
        em.op(DV, lambda e, x=xsl, c=sw_col: e.tensor_scalar(
            scr_a[:, :], x, st[:, c:c + 1], None, ALU.mult),
            reads=(xres, "st"), writes=("scr_a",))
        em.op(DV, lambda e: e.tensor_single_scalar(
            scr_b[:, :], scr_a[:, :], 0.5, ALU.is_ge),
            reads=("scr_a",), writes=("scr_b",))
        em.op(DV, lambda e: e.scalar_tensor_tensor(
            tbf[:, :], scr_a[:, :], -0.5, scr_b[:, :],
            ALU.is_lt, ALU.subtract),
            reads=("scr_a", "scr_b"), writes=("tbf",))

    w_mean_pass(w1_h, CC * D, 16)
    w_scale_finish(16, 896, 22, 23)
    em.op(DV, lambda e: e.tensor_scalar(
        cwt[:, :], cwt[:, :], st[:, 23:24], None, ALU.mult),
        reads=("cwt", "st"), writes=("cwt",))
    for cc in range(CC):
        tern_chunk(w1_h, cc, 22)
        for dc in range(HC):
            src = tbf[:, dc * P:(dc + 1) * P]
            dst = qTw[:, dc * 512 + (cc % 4) * P: dc * 512 + (cc % 4 + 1) * P]
            pe_transpose(src, dst,
                         reads=("tbf",), writes=("qTw",))
        if cc % 4 == 3:
            em.dma(SY, scr_ap(w1_s, cc // 4, HC * 512), qTw[:, 0:HC * 512],
                   reads=("qTw",))

    w_mean_pass(w2_h, HC * DH, 32)
    w_scale_finish(32, 900, 40, 41)
    for dc in range(HC):
        for g in range(DH // D):
            tern_chunk(w2_h, dc * 4 + g, 40)
            for k in range(8):
                cc = g * 8 + k
                src = tbf[:, k * P:(k + 1) * P]
                dst = qTw[:, cc * P:(cc + 1) * P]
                pe_transpose(src, dst,
                             reads=("tbf",), writes=("qTw",))
        em.dma(SY, scr_ap(w2_s, dc, CC * P), qTw[:, 0:CC * P],
               reads=("qTw",))

    for sq in range(NSEQ):
        em.region = "attn"
        for w in range(NW):
            w0 = w * WIN
            for tt in range(WT):
                ti = w0 // P + tt
                xres = "xt%d" % (tt % 2)
                xs = xtile[:, (tt % 2) * D:(tt % 2 + 1) * D]
                em.dma(SY, xs, x_h[sq, ti * P:(ti + 1) * P, :],
                       writes=(xres,))
                ln_stats(xs, xres)
                em.op(DV, lambda e, x=xs: e.tensor_scalar(
                    tbf[:, :], x, st[:, 2:3], st[:, 6:7],
                    ALU.subtract, ALU.mult),
                    reads=(xres, "st"), writes=("tbf",))
                for hc in range(HC):
                    src = tbf[:, hc * P:(hc + 1) * P]
                    dst = xlT[:, hc * WIN + tt * P: hc * WIN + (tt + 1) * P]
                    pe_transpose(src, dst,
                                 reads=("tbf",), writes=("xlT",))
            for wi in range(3):
                ld = em.dma(SY, wring[:, 0:HC * D], scr_ap(aw_s, wi, HC * D),
                            writes=("wring0", "wring1"))
                if wi < 2:
                    gtab = G if wi == 0 else Gi
                    gres = "G" if wi == 0 else "Gi"
                    boff = 0 if wi == 0 else 8
                    for oc in range(HC):
                        ps, pres = (pA, "pA") if oc % 2 == 0 else (pB, "pB")
                        ps_ap = ps[:, 0:WIN]
                        for hc in range(HC):
                            lhs = wring[:, hc * D + oc * P:
                                        hc * D + (oc + 1) * P]
                            rhs = xlT[:, hc * WIN:(hc + 1) * WIN]
                            em.op(TE, lambda e, o=ps_ap, l=lhs, r=rhs, h=hc:
                                  e.matmul(o, l, r, start=(h == 0),
                                           stop=(h == HC - 1)),
                                  reads=("wring0", "wring1", "xlT"),
                                  writes=(pres,), waits=[ld])
                        if wi == 0:
                            dst, dres = qTw[:, oc * WIN:(oc + 1) * WIN], "qTw"
                        else:
                            dst = kT[:, oc * S + w0: oc * S + w0 + WIN]
                            dres = "kT"
                        gsl = gtab[:, w0:w0 + WIN]
                        bcol = bqk[:, boff + oc: boff + oc + 1]
                        em.op(DV, lambda e, d=dst, p=ps_ap, g=gsl, b=bcol:
                              e.scalar_tensor_tensor(d, p, b, g, ALU.add,
                                                     ALU.mult),
                              reads=(pres, gres, "consts"), writes=(dres,))
                else:
                    for tt in range(WT):
                        ti = w0 // P + tt
                        for half in range(2):
                            ps, pres = (pA, "pA") if half == 0 else (pB, "pB")
                            ps_ap = ps[:, 0:512]
                            for hc in range(HC):
                                lhs = xlT[:, hc * WIN + tt * P:
                                          hc * WIN + (tt + 1) * P]
                                rhs = wring[:, hc * D + half * 512:
                                            hc * D + (half + 1) * 512]
                                em.op(TE, lambda e, o=ps_ap, l=lhs, r=rhs,
                                      h=hc: e.matmul(o, l, r,
                                                     start=(h == 0),
                                                     stop=(h == HC - 1)),
                                      reads=("wring0", "wring1", "xlT"),
                                      writes=(pres,), waits=[ld])
                            dst = vT[:, ti * D + half * 512:
                                     ti * D + (half + 1) * 512]
                            em.op(SC, lambda e, d=dst, p=ps_ap:
                                  e.activation(d, p, ACT.Copy, bias=0.0,
                                               scale=1.0),
                                  reads=(pres, "consts"), writes=("vT",))
            njt = w0 // P + WT
            for j in range(njt):
                psh, b0, pres = ((pA, 0, "pA"), (pB, 0, "pB"),
                                 (pP, 0, "pPl"), (pP, 512, "pPh"))[j % 4]
                lo = max(0, j * P - w0)
                ncol = WIN - lo
                ps_ap = psh[:, b0:b0 + ncol]
                for hc in range(HC):
                    lhs = kT[:, hc * S + j * P: hc * S + (j + 1) * P]
                    rhs = qTw[:, hc * WIN + lo:(hc + 1) * WIN]
                    em.op(TE, lambda e, o=ps_ap, l=lhs, r=rhs, h=hc: e.matmul(
                        o, l, r, start=(h == 0), stop=(h == HC - 1)),
                        reads=("kT", "qTw"), writes=(pres,))
                sres = "hq%d" % j
                if j * P >= w0:
                    m_dst = hq[:, j * 512 + lo: j * 512 + lo + P]
                    em.op(DV, lambda e, d=m_dst, p=psh[:, b0:b0 + P]:
                          e.tensor_mul(d, p, msk[:, :]),
                          reads=(pres, "consts"), writes=(sres,))
                    if ncol > P:
                        r_dst = hq[:, j * 512 + lo + P: j * 512 + WIN]
                        em.op(SC, lambda e, d=r_dst,
                              p=psh[:, b0 + P:b0 + ncol]:
                              e.activation(d, p, ACT.Copy, bias=0.0,
                                           scale=1.0),
                              reads=(pres, "consts"), writes=(sres,))
                else:
                    strip = hq[:, j * 512 + lo: j * 512 + WIN]
                    em.op(SC, lambda e, d=strip, p=ps_ap:
                          e.activation(d, p, ACT.Copy, bias=0.0, scale=1.0),
                          reads=(pres, "consts"), writes=(sres,))
            for tt in range(WT):
                i = w0 // P + tt
                xres = "xt%d" % (tt % 2)
                xs = xtile[:, (tt % 2) * D:(tt % 2 + 1) * D]
                em.dma(SY, xs, x_h[sq, i * P:(i + 1) * P, :], writes=(xres,))
                for j in range(i + 1):
                    lhs = hq[:, j * 512 + tt * P: j * 512 + (tt + 1) * P]
                    for half in range(2):
                        ps_ap = pO[:, half * 512:(half + 1) * 512]
                        pres = "pOl" if half == 0 else "pOh"
                        rhs = vT[:, j * D + half * 512:
                                 j * D + (half + 1) * 512]
                        em.op(TE, lambda e, o=ps_ap, l=lhs, r=rhs, jj=j,
                              ii=i: e.matmul(o, l, r, start=(jj == 0),
                                             stop=(jj == ii)),
                              reads=("hq%d" % j, "vT"), writes=(pres,))
                ln_stats(pO[:, :], "pOl", psum_src=True,
                         extra_reads=("pOh",))
                em.op(DV, lambda e: e.tensor_scalar(
                    scr_a[:, :], pO[:, :], st[:, 2:3], st[:, 6:7],
                    ALU.subtract, ALU.mult),
                    reads=("pOl", "pOh", "st"), writes=("scr_a",))
                dst = x2[:, i * D:(i + 1) * D]
                em.op(DV, lambda e, d=dst, x=xs: e.tensor_add(
                    d, scr_a[:, :], x),
                    reads=("scr_a", xres), writes=("x2",))
        em.region = "ffn"
        for i in range(NT):
            xsl = x2[:, i * D:(i + 1) * D]
            ln_stats(xsl, "x2")
            em.op(DV, lambda e, x=xsl: e.tensor_scalar(
                scr_a[:, :], x, st[:, 2:3], st[:, 6:7],
                ALU.subtract, ALU.mult),
                reads=("x2", "st"), writes=("scr_a",))
            em.op(DV, lambda e: e.reduce_max(st[:, 7:8], scr_a[:, :], AX,
                                             apply_absolute_value=True),
                  reads=("scr_a",), writes=("st",))
            em.op(DV, lambda e: e.tensor_single_scalar(
                st[:, 7:8], st[:, 7:8], 1e-5, ALU.max),
                reads=("st",), writes=("st",))
            em.op(DV, lambda e: e.reciprocal(st[:, 8:9], st[:, 7:8]),
                  reads=("st",), writes=("st",))
            em.op(DV, lambda e: e.tensor_single_scalar(
                st[:, 8:9], st[:, 8:9], 127.0, ALU.mult),
                reads=("st",), writes=("st",))
            em.op(DV, lambda e: e.reciprocal(st[:, 9:10], st[:, 8:9]),
                  reads=("st",), writes=("st",))
            em.op(SC, lambda e: e.activation(
                scr_b[:, :], scr_a[:, :], ACT.Copy, bias=0.0,
                scale=st[:, 8:9]),
                reads=("scr_a", "st", "consts"), writes=("scr_b",))
            em.op(DV, lambda e: e.tensor_scalar(
                u32[:, :], scr_b[:, :], 0.5, None, ALU.add),
                reads=("scr_b",), writes=("u32",))
            em.op(DV, lambda e: e.tensor_copy(scr_c[:, :], u32[:, :]),
                  reads=("u32",), writes=("scr_c",))
            em.op(DV, lambda e: e.scalar_tensor_tensor(
                scr_d[:, :], scr_b[:, :], 0.5, scr_c[:, :],
                ALU.add, ALU.is_lt),
                reads=("scr_b", "scr_c"), writes=("scr_d",))
            em.op(DV, lambda e: e.scalar_tensor_tensor(
                scr_b[:, :], scr_d[:, :], -1.0, scr_c[:, :],
                ALU.mult, ALU.add),
                reads=("scr_d", "scr_c"), writes=("scr_b",))
            em.op(DV, lambda e: e.tensor_scalar(
                tbf[:, :], scr_b[:, :], st[:, 9:10], None, ALU.mult),
                reads=("scr_b", "st"), writes=("tbf",))
            for hc in range(HC):
                src = tbf[:, hc * P:(hc + 1) * P]
                dst = kT[:, hc * S + i * P: hc * S + (i + 1) * P]
                pe_transpose(src, dst,
                             reads=("tbf",), writes=("kT",))
        for b in range(NBLK):
            em.region = "blk"
            t0 = b * BLK
            lo = t0 - 1
            clo, chi = max(lo, 0), min(t0 + BLK + 1, S)
            mid = t0 + BLK // 2
            lenA, lenB = mid - clo, chi - mid
            for cc in range(CC):
                ccg, sl4 = cc // 4, cc % 4
                if sl4 == 0:
                    w1ld = em.dma(SY, wring[:, 0:HC * 512],
                                  scr_ap(w1_s, ccg, HC * 512),
                                  writes=("wring0",))
                hres = "hs%d" % (cc % 2)
                hs = hstr[:, (cc % 2) * 516:(cc % 2) * 516 + 516]
                for (ps, pres, c0, ln) in ((pA, "pA", clo, lenA),
                                           (pB, "pB", mid, lenB)):
                    ps_ap = ps[:, 0:ln]
                    for dc in range(HC):
                        lhs = wring[:, dc * 512 + sl4 * P:
                                    dc * 512 + (sl4 + 1) * P]
                        rhs = kT[:, dc * S + c0: dc * S + c0 + ln]
                        em.op(TE, lambda e, o=ps_ap, l=lhs, r=rhs, h=dc:
                              e.matmul(o, l, r, start=(h == 0),
                                       stop=(h == HC - 1)),
                              reads=("wring0", "kT"), writes=(pres,),
                              waits=[w1ld])
                if clo > lo:
                    em.op(DV, lambda e, h=hs: e.memset(h[:, 0:1], 0.0),
                          reads=(), writes=(hres,))
                if chi < t0 + BLK + 1:
                    em.op(DV, lambda e, h=hs, c0_=chi - lo: e.memset(
                        h[:, c0_:BLK + 2], 0.0), reads=(), writes=(hres,))
                d_a = hs[:, clo - lo: clo - lo + lenA]
                d_b = hs[:, mid - lo: mid - lo + lenB]
                em.op(SC, lambda e, d=d_a, p=pA[:, 0:lenA]: e.activation(
                    d, p, ACT.Copy, bias=0.0, scale=1.0),
                    reads=("pA", "consts"), writes=(hres,))
                em.op(SC, lambda e, d=d_b, p=pB[:, 0:lenB]: e.activation(
                    d, p, ACT.Copy, bias=0.0, scale=1.0),
                    reads=("pB", "consts"), writes=(hres,))
                c0c = cwt[:, cc * 3 + 0: cc * 3 + 1]
                c1c = cwt[:, cc * 3 + 1: cc * 3 + 2]
                c2c = cwt[:, cc * 3 + 2: cc * 3 + 3]
                em.op(SC, lambda e, h=hs, c=c0c: e.activation(
                    scr_a[:, 0:BLK], h[:, 0:BLK], ACT.Copy, bias=0.0,
                    scale=c),
                    reads=(hres, "cwt", "consts"), writes=("scr_a",))
                em.op(DV, lambda e, h=hs, c=c1c: e.scalar_tensor_tensor(
                    scr_b[:, 0:BLK], h[:, 1:BLK + 1], c, scr_a[:, 0:BLK],
                    ALU.mult, ALU.add),
                    reads=(hres, "scr_a", "cwt"), writes=("scr_b",))
                em.op(DV, lambda e, h=hs, c=c2c: e.scalar_tensor_tensor(
                    scr_a[:, 0:BLK], h[:, 2:BLK + 2], c, scr_b[:, 0:BLK],
                    ALU.mult, ALU.add),
                    reads=(hres, "scr_b", "cwt"), writes=("scr_a",))
                dst = hq[:, cc * BLK:(cc + 1) * BLK]
                em.op(SC, lambda e, d=dst, c=cbt[:, cc:cc + 1]: e.activation(
                    d, scr_a[:, 0:BLK], ACT.Silu, bias=c, scale=1.0),
                    reads=("scr_a", "consts"), writes=("hq%d" % cc,))
                if cc % 8 == 7:
                    g = cc // 8
                    grp = tuple("hq%d" % c for c in range(g * 8, cc + 1))
                    rdst = scr_d if g == 0 else scr_c
                    em.op(DV, lambda e, d=rdst, g0=g * 8: e.reduce_max(
                        d[:, 0:BLK, None],
                        hq[:, g0 * BLK:(g0 + 8) * BLK].rearrange(
                            "p (c t) -> p t c", c=8), AX,
                        apply_absolute_value=True),
                        reads=grp, writes=("scr_d" if g == 0 else "scr_c",))
                    if g > 0:
                        em.op(DV, lambda e: e.tensor_max(
                            scr_d[:, 0:BLK], scr_d[:, 0:BLK],
                            scr_c[:, 0:BLK]),
                            reads=("scr_c", "scr_d"), writes=("scr_d",))
            em.op(DV, lambda e: e.tensor_copy(M_s[:, 0:BLK], scr_d[:, 0:BLK]),
                  reads=("scr_d",), writes=("M",))
            for q4 in range(BT):
                src = M_s[:, q4 * P:(q4 + 1) * P]
                sl = tp_slot[0]
                tp_slot[0] ^= 1
                ps_ap = pS[:, sl * 512: sl * 512 + P]
                pres = "pSA" if sl == 0 else "pSB"
                em.op(TE, lambda e, s=src, p=ps_ap: e.matmul(
                    p, s, idb[:, :], start=True, stop=True),
                    reads=("M", "consts"), writes=(pres,))
                em.op(DV, lambda e, p=ps_ap, q=q4: e.reduce_max(
                    st[:, 10 + q:11 + q], p, AX),
                    reads=(pres,), writes=("st",))
            em.op(DV, lambda e: e.tensor_single_scalar(
                st[:, 10:10 + BT], st[:, 10:10 + BT], 1e-5, ALU.max),
                reads=("st",), writes=("st",))
            em.op(DV, lambda e: e.reciprocal(st[:, 14:14 + BT],
                                             st[:, 10:10 + BT]),
                  reads=("st",), writes=("st",))
            em.op(DV, lambda e: e.tensor_single_scalar(
                st[:, 14:14 + BT], st[:, 14:14 + BT], 127.0, ALU.mult),
                reads=("st",), writes=("st",))
            em.op(DV, lambda e: e.reciprocal(st[:, 18:18 + BT],
                                             st[:, 14:14 + BT]),
                  reads=("st",), writes=("st",))
            em.op(DV, lambda e: e.tensor_scalar(
                st[:, 44:44 + BT], st[:, 18:18 + BT], st[:, 41:42],
                None, ALU.mult), reads=("st",), writes=("st",))
            sl = tp_slot[0]
            tp_slot[0] ^= 1
            ps_ap = pS[0:BT, sl * 512: sl * 512 + P]
            pres = "pSA" if sl == 0 else "pSB"
            em.op(TE, lambda e, p=ps_ap: e.matmul(
                p, st[:, 14:14 + BT], idf[:, :], start=True, stop=True),
                reads=("st", "consts"), writes=(pres,))
            em.op(DV, lambda e, p=ps_ap: e.tensor_copy(sxT[0:BT, :], p),
                  reads=(pres,), writes=("sxT",))
            for q4 in range(BT):
                ps_r = pS[:, 768:768 + P]
                em.op(DV, lambda e, r=q4: e.tensor_scalar(
                    sxm[0:BT, :], sxT[0:BT, :],
                    idf[0:BT, r:r + 1], None, ALU.mult),
                    reads=("sxT", "consts"), writes=("sxm",))
                em.op(TE, lambda e, p=ps_r: e.matmul(
                    p, on8[0:BT, :], sxm[0:BT, :],
                    start=True, stop=True),
                    reads=("sxm", "consts"), writes=("pSB",))
                em.op(SC, lambda e, p=ps_r, q=q4:
                      e.activation(Row[:, q * P:(q + 1) * P], p, ACT.Copy,
                                   bias=0.0, scale=1.0),
                      reads=("pSB", "consts"), writes=("Row",))
            for cc in range(CC):
                hres = "hq%d" % cc
                ssl = hq[:, cc * BLK:(cc + 1) * BLK]
                em.op(DV, lambda e, s=ssl: e.tensor_mul(
                    scr_a[:, 0:BLK], s, Row[:, 0:BLK]),
                    reads=(hres, "Row"), writes=("scr_a",))
                em.op(DV, lambda e: e.tensor_scalar(
                    u32[:, 0:BLK], scr_a[:, 0:BLK], 0.5, None, ALU.add),
                    reads=("scr_a",), writes=("u32",))
                em.op(DV, lambda e: e.tensor_copy(scr_c[:, 0:BLK],
                                                  u32[:, 0:BLK]),
                      reads=("u32",), writes=("scr_c",))
                em.op(DV, lambda e: e.scalar_tensor_tensor(
                    scr_b[:, 0:BLK], scr_a[:, 0:BLK], 0.5, scr_c[:, 0:BLK],
                    ALU.add, ALU.is_lt),
                    reads=("scr_a", "scr_c"), writes=("scr_b",))
                em.op(DV, lambda e, s=ssl: e.scalar_tensor_tensor(
                    s, scr_b[:, 0:BLK], -1.0, scr_c[:, 0:BLK],
                    ALU.mult, ALU.add),
                    reads=("scr_b", "scr_c"), writes=(hres,))
            psums = [(pA[:, 0:BLK], "pA"), (pB[:, 0:BLK], "pB"),
                     (pO[:, 0:BLK], "pOl"), (pO[:, 512:512 + BLK], "pOh"),
                     (pP[:, 0:BLK], "pPl"), (pP[:, 512:512 + BLK], "pPh"),
                     (pS[:, 0:BLK], "pSA"), (pS[:, 512:512 + BLK], "pSB")]
            for dc in range(HC):
                w2ld = em.dma(SY, wring[:, 4096:4096 + CC * P],
                              scr_ap(w2_s, dc, CC * P), writes=("wring1",))
                ps_ap, pres = psums[dc]
                for cc in range(CC):
                    lhs = wring[:, 4096 + cc * P:4096 + (cc + 1) * P]
                    rhs = hq[:, cc * BLK:(cc + 1) * BLK]
                    em.op(TE, lambda e, o=ps_ap, l=lhs, r=rhs, c=cc:
                          e.matmul(o, l, r, start=(c == 0),
                                   stop=(c == CC - 1)),
                          reads=("wring1", "hq%d" % cc), writes=(pres,),
                          waits=[w2ld])
            for dc in range(HC):
                em.op(SC, lambda e, d=hq[:, dc * BLK:(dc + 1) * BLK],
                      p=psums[dc][0]:
                      e.activation(d, p, ACT.Copy, bias=0.0, scale=1.0),
                      reads=(psums[dc][1], "consts"),
                      writes=("y2s", "hq%d" % dc))
            for q4 in range(BT):
                i = t0 // P + q4
                for dc in range(HC):
                    src = hq[:, dc * BLK + q4 * P: dc * BLK + (q4 + 1) * P]
                    dst = scr_d[:, dc * P:(dc + 1) * P]
                    pe_transpose(src, dst, reads=("y2s",), writes=("scr_d",),
                                 scale=st[:, 44 + q4:45 + q4], scres=("st",))
                ln_stats(scr_d[:, :], "scr_d")
                em.op(DV, lambda e: e.tensor_scalar(
                    scr_a[:, :], scr_d[:, :], st[:, 2:3], st[:, 6:7],
                    ALU.subtract, ALU.mult),
                    reads=("scr_d", "st"), writes=("scr_a",))
                em.op(DV, lambda e, x=x2[:, i * D:(i + 1) * D]:
                      e.tensor_add(scr_b[:, :], scr_a[:, :], x),
                      reads=("scr_a", "x2"), writes=("scr_b",))
                em.dma(SC, y_h[sq, i * P:(i + 1) * P, :], scr_b[:, :],
                       reads=("scr_b",))

    em._raw_op(SY, None, em.dma_all(SC) + em.dma_all(SY) + em.dma_all(GP),
               ())

    block = ctx.enter_context(nc.Block())
    with nc.allow_low_precision(reason="bf16 matmul staging; fp32 accum"):
        em.emit(block)
    ctx.close()
    return nc


def _host_pack(Wq, bq, Wk, bk, Wv, w1, conv_w, conv_b, w2):
    f32 = np.float32
    pk = {}
    pk["wq"] = np.ascontiguousarray(
        np.asarray(Wq, f32).reshape(HC, P, D).transpose(1, 0, 2)
        .reshape(P, HC * D))
    pk["wk"] = np.ascontiguousarray(
        np.asarray(Wk, f32).reshape(HC, P, D).transpose(1, 0, 2)
        .reshape(P, HC * D))
    pk["wv"] = np.ascontiguousarray(
        np.asarray(Wv, f32).reshape(HC, P, D).transpose(1, 0, 2)
        .reshape(P, HC * D))
    bqk = np.zeros((P, 16), f32)
    bqk[:, 0:8] = np.asarray(bq, f32).reshape(HC, P).T
    bqk[:, 8:16] = np.asarray(bk, f32).reshape(HC, P).T
    pk["bqk"] = bqk
    pk["w1"] = np.ascontiguousarray(
        np.asarray(w1, f32).reshape(CC, P, D).transpose(1, 0, 2)
        .reshape(P, CC * D))
    pk["w2"] = np.ascontiguousarray(
        np.asarray(w2, f32).reshape(HC, P, DH).transpose(1, 0, 2)
        .reshape(P, HC * DH))
    pk["cw"] = np.ascontiguousarray(
        np.asarray(conv_w, f32).reshape(CC, P, 3).transpose(1, 0, 2)
        .reshape(P, CC * 3))
    pk["cb"] = np.ascontiguousarray(np.asarray(conv_b, f32).reshape(CC, P).T)
    pk["idb"] = np.eye(P, dtype=ml_dtypes.bfloat16)
    pk["idf"] = np.eye(P, dtype=np.float32)
    pk["msk"] = (np.arange(P)[None, :] >= np.arange(P)[:, None]).astype(
        ml_dtypes.bfloat16)
    pk["onf"] = np.ones((1, P), np.float32)
    pk["on8"] = np.ones((8, P), np.float32)
    pk["onc"] = np.ones((P, 1), np.float32)
    pk["cst"] = np.tile(np.array([[EPS, 0.0, 1.0, 0, 0, 0, 0, 0]],
                                 np.float32), (P, 1))
    return pk


def kernel(L0_attn_x, L0_attn_Wq, L0_attn_bq, L0_attn_Wk, L0_attn_bk,
           L0_attn_Wv, L0_ffn_w1, L0_ffn_conv_w, L0_ffn_conv_b, L0_ffn_w2):
    B, S, _ = L0_attn_x.shape
    NSEQ = B // NCORES
    pk = _host_pack(L0_attn_Wq, L0_attn_bq, L0_attn_Wk, L0_attn_bk,
                    L0_attn_Wv, L0_ffn_w1, L0_ffn_conv_w, L0_ffn_conv_b,
                    L0_ffn_w2)
    nc = build_nc(NSEQ, S, xdv=())
    x = np.ascontiguousarray(np.asarray(L0_attn_x, np.float32))
    in_maps = []
    for c in range(NCORES):
        m = dict(pk)
        m["x"] = x[c * NSEQ:(c + 1) * NSEQ]
        in_maps.append(m)
    res = run_bass_kernel_spmd(nc, in_maps, core_ids=list(range(NCORES)))
    out = np.concatenate([np.asarray(r["y"]) for r in res.results], axis=0)
    return out.astype(np.float32, copy=False)


# version 12
# speedup vs baseline: 1.0140x; 1.0090x over previous
import numpy as np
import ml_dtypes
import concourse.bass as bass
import concourse.mybir as mybir
from concourse.bass_utils import run_bass_kernel_spmd

F32 = mybir.dt.float32
BF16 = mybir.dt.bfloat16
I32 = mybir.dt.int32
AX = mybir.AxisListType.X
ALU = mybir.AluOpType
ACT = mybir.ActivationFunctionType

P = 128
D = 1024
DH = 4096
HC = D // P
CC = DH // P
GAMMA = 0.96875
EPS = 1e-5
NCORES = 8


class Em:

    ENGINES = ("sync", "gpsimd", "tensor", "vector", "scalar")

    def __init__(self, xdv=()):
        self.xdv = set(xdv)
        self.region = "pro"
        self.q = {e: [] for e in self.ENGINES}
        self.cnt = {}
        self.esem = {}
        self.wr = {}
        self.rd = {}
        self.psum_res = set()

    def _xtra(self, eng=None):
        glue = "glue" in self.xdv and eng != "tensor"
        ls = ("ls_" + self.region) in self.xdv or "ls" in self.xdv
        out = ()
        if glue or ls:
            out = tuple((sm, self.cnt[sm[1]]) for sm in self.esem.values())
            for pl in self.pool.values():
                out = out + tuple((sm, self.cnt[sm[1]]) for sm in pl)
        elif eng == "tensor":
            if "gt_d" in self.xdv:
                for pl in self.pool.values():
                    out = out + tuple((sm, self.cnt[sm[1]]) for sm in pl)
            if "gt_e" in self.xdv:
                out = out + tuple((sm, self.cnt[sm[1]])
                                  for sm in self.esem.values())
        return out

    def sem(self, handle, name):
        self.cnt[name] = 0
        return (handle, name)

    def val(self, sem):
        return self.cnt[sem[1]]

    def _raw_op(self, eng, fn, waits, incs):
        for (s, v) in waits:
            assert v <= self.cnt[s[1]], \
                f"forward wait {s[1]}>={v} emitted={self.cnt[s[1]]}"
        for (s, n) in incs:
            self.cnt[s[1]] += n
        self.q[eng].append((fn, tuple(waits), tuple(incs)))

    def _collect(self, own, reads, writes, waits):
        w = {}

        def add(tok):
            if tok is None:
                return
            s, v = tok
            if own is not None and s[1] == own[1]:
                return
            if s[1] in w:
                w[s[1]] = (s, max(v, w[s[1]][1]))
            else:
                w[s[1]] = (s, v)

        for r in reads:
            add(self.wr.get(r))
            if r in self.psum_res:
                for tok in self.rd.get(r, {}).values():
                    add(tok)
        for r in writes:
            add(self.wr.get(r))
            for tok in self.rd.get(r, {}).values():
                add(tok)
        for tok in waits:
            add(tok)
        return list(w.values())

    def _post(self, tok, reads, writes, key):
        for r in writes:
            self.wr[r] = tok
            self.rd[r] = {}
        for r in reads:
            self.rd.setdefault(r, {})[key] = tok

    def op(self, eng, fn, reads=(), writes=(), waits=()):
        es = self.esem[eng]
        waits = tuple(waits) + self._xtra(eng)
        w = self._collect(None, reads, writes, waits)
        self._raw_op(eng, fn, w, [(es, 1)])
        tok = (es, self.cnt[es[1]])
        self._post(tok, reads, writes, es[1])
        return tok

    def set_pool(self, eng, sems):
        if not hasattr(self, "pool"):
            self.pool = {}
            self.pool_n = {}
        self.pool[eng] = sems
        self.pool_n[eng] = 0

    def dma(self, eng, out_ap, in_ap, reads=(), writes=(), waits=()):
        pool = self.pool[eng]
        n = self.pool_n[eng]
        sm = pool[n % len(pool)]
        prev = 16 * (n // len(pool))
        waits = tuple(waits) + self._xtra(eng)
        w = self._collect(None, reads, writes,
                          tuple(waits) + ((sm, prev),))
        self._raw_op(eng, lambda e, o=out_ap, i=in_ap:
                     e.dma_start(out=o, in_=i), w, [(sm, 16)])
        self.pool_n[eng] = n + 1
        tok = (sm, prev + 16)
        self._post(tok, reads, writes, sm[1])
        return tok

    def dma_all(self, eng):
        return [(sm, self.cnt[sm[1]]) for sm in self.pool[eng]]

    def emit(self, block):
        def body(eng_obj, name):
            hwm = {}
            for (fn, waits, incs) in self.q[name]:
                for (s, v) in waits:
                    if v > 0 and v > hwm.get(s[1], 0):
                        eng_obj.wait_ge(s[0], v)
                        hwm[s[1]] = v
                ins = fn(eng_obj) if fn is not None else None
                for (s, n) in incs:
                    if ins is None:
                        eng_obj.sem_inc(s[0], n)
                    else:
                        ins.then_inc(s[0], n)
        block.sync(lambda e: body(e, "sync"))
        block.gpsimd(lambda e: body(e, "gpsimd"))
        block.tensor(lambda e: body(e, "tensor"))
        block.vector(lambda e: body(e, "vector"))
        block.scalar(lambda e: body(e, "scalar"))


def build_nc(NSEQ, S, xdv=()):
    NT = S // P
    NW = max(1, S // 512)
    WIN = S // NW
    WT = WIN // P
    NBLK = max(1, S // 512)
    BLK = S // NBLK
    BT = BLK // P

    nc = bass.Bass("TRN2", target_bir_lowering=False,
                   detect_race_conditions=False,
                   dynamic_dma_scratch_size=8192)

    x_h = nc.dram_tensor("x", [NSEQ, S, D], F32, kind="ExternalInput")
    y_h = nc.dram_tensor("y", [NSEQ, S, D], F32, kind="ExternalOutput")
    wq_h = nc.dram_tensor("wq", [P, HC * D], F32, kind="ExternalInput")
    wk_h = nc.dram_tensor("wk", [P, HC * D], F32, kind="ExternalInput")
    wv_h = nc.dram_tensor("wv", [P, HC * D], F32, kind="ExternalInput")
    bqk_h = nc.dram_tensor("bqk", [P, 16], F32, kind="ExternalInput")
    w1_h = nc.dram_tensor("w1", [P, CC * D], F32, kind="ExternalInput")
    w2_h = nc.dram_tensor("w2", [P, HC * DH], F32, kind="ExternalInput")
    cw_h = nc.dram_tensor("cw", [P, 3 * CC], F32, kind="ExternalInput")
    cb_h = nc.dram_tensor("cb", [P, CC], F32, kind="ExternalInput")
    idb_h = nc.dram_tensor("idb", [P, P], BF16, kind="ExternalInput")
    idf_h = nc.dram_tensor("idf", [P, P], F32, kind="ExternalInput")
    mk_h = nc.dram_tensor("msk", [P, P], BF16, kind="ExternalInput")
    onf_h = nc.dram_tensor("onf", [1, P], F32, kind="ExternalInput")
    on8_h = nc.dram_tensor("on8", [8, P], F32, kind="ExternalInput")
    onc_h = nc.dram_tensor("onc", [P, 1], F32, kind="ExternalInput")
    cst_h = nc.dram_tensor("cst", [P, 8], F32, kind="ExternalInput")

    aw_s = nc.dram_tensor("aw_s", [3, P, HC * D], BF16, kind="Internal")
    w1_s = nc.dram_tensor("w1_s", [CC // 4, P, HC * 512], BF16,
                          kind="Internal")
    w2_s = nc.dram_tensor("w2_s", [HC, P, CC * P], BF16, kind="Internal")

    def scr_ap(t, row, ncols, coff=0, ccnt=None):
        return bass.AP(t, row * P * ncols + coff,
                       [[ncols, P], [1, ccnt if ccnt is not None else ncols]])

    from contextlib import ExitStack
    ctx = ExitStack()
    sb = lambda n, s, d: ctx.enter_context(nc.sbuf_tensor(n, s, d))
    psf = lambda n, s: ctx.enter_context(nc.psum_tensor(n, s, F32))

    kT = sb("kT", [P, HC * S], BF16)
    vT = sb("vT", [P, NT * D], BF16)
    x2 = sb("x2", [P, NT * D], BF16)
    hq = sb("hq", [P, CC * BLK], BF16)
    wring = sb("wring", [P, 8192], BF16)
    qTw = sb("qTw", [P, max(HC * WIN, 4096)], BF16)
    xlT = sb("xlT", [P, HC * WIN], BF16)
    xtile = sb("xtile", [P, 2 * D], F32)
    hstr = sb("hstr", [P, 2 * 516], F32)
    scr_a = sb("scr_a", [P, D], F32)
    scr_b = sb("scr_b", [P, D], F32)
    scr_c = sb("scr_c", [P, D], F32)
    scr_d = sb("scr_d", [P, D], F32)
    scjunk = sb("scjunk", [P, D], F32)
    u32 = sb("u32", [P, D], I32)
    M_s = sb("M_s", [P, 512], BF16)
    Row = sb("Row", [P, 512], F32)
    Rowo = sb("Rowo", [P, 512], F32)
    G = sb("G", [P, S], BF16)
    Gi = sb("Gi", [P, S], BF16)
    st = sb("st", [P, 64], F32)
    sxT = sb("sxT", [8, P], F32)
    idb = sb("idb_s", [P, P], BF16)
    idf = sb("idf_s", [P, P], F32)
    msk = sb("msk_s", [P, P], BF16)
    onf = sb("onf_s", [1, P], F32)
    on8 = sb("on8_s", [8, P], F32)
    cst = sb("cst_s", [P, 8], F32)
    sxm = sb("sxm_s", [8, P], F32)
    onc = sb("onc_s", [P, 1], F32)
    bqk = sb("bqk_s", [P, 16], F32)
    cwt = sb("cw_s", [P, 3 * CC], F32)
    cbt = sb("cb_s", [P, CC], F32)
    tbf = sb("tbf", [P, D], BF16)

    pA = psf("pA", [P, 512])
    pB = psf("pB", [P, 512])
    pO = psf("pO", [P, 1024])
    pP = psf("pP", [P, 1024])
    pS = psf("pS", [P, 1024])

    sem_h = lambda n: ctx.enter_context(nc.semaphore(n))
    em = Em(xdv=xdv)
    for eng, nm in (("tensor", "s_te"), ("vector", "s_dv"),
                    ("scalar", "s_sc"), ("sync", "s_sy"),
                    ("gpsimd", "s_gq")):
        em.esem[eng] = em.sem(sem_h(nm), nm)
    em.set_pool("sync", [em.sem(sem_h(f"dqs{i}"), f"dqs{i}")
                         for i in range(8)])
    em.set_pool("gpsimd", [em.sem(sem_h(f"dqg{i}"), f"dqg{i}")
                           for i in range(2)])
    em.set_pool("scalar", [em.sem(sem_h(f"dqo{i}"), f"dqo{i}")
                           for i in range(4)])

    TE, DV, SC, SY, GP = "tensor", "vector", "scalar", "sync", "gpsimd"
    for r in ("pA", "pB", "pOl", "pOh", "pPl", "pPh", "pSA", "pSB"):
        em.psum_res.add(r)

    for ci, cv in enumerate((EPS, 0.0, 1.0)):
        nc.const_aps.register(F32, cv, cst[:, ci:ci + 1])

    tp_slot = [0]

    def pe_transpose(src_ap, dst_ap, reads, writes, f32mode=False, waits=(),
                     scale=None, scres=()):
        sl = tp_slot[0]
        tp_slot[0] ^= 1
        ps_ap = pS[:, sl * 512: sl * 512 + P]
        pres = "pSA" if sl == 0 else "pSB"
        ident = idf if f32mode else idb
        em.op(TE, lambda e, s=src_ap, p=ps_ap, i=ident: e.matmul(
            p, s, i[:, :], start=True, stop=True),
            reads=tuple(reads) + ("consts",), writes=(pres,), waits=waits)
        em.op(SC, lambda e, p=ps_ap, d=dst_ap, sc=scale: e.activation(
            d, p, ACT.Copy, bias=0.0, scale=(1.0 if sc is None else sc)),
            reads=(pres, "consts") + tuple(scres), writes=tuple(writes))

    def ln_stats(src_ap, src_res, psum_src=False, extra_reads=()):
        rr = (src_res,) + tuple(extra_reads)
        rc_ = rr + ("consts",)
        if psum_src:
            em.op(DV, lambda e, x=src_ap: e.reduce_sum(st[:, 0:1], x, AX),
                  reads=rr, writes=("st",))
            em.op(SC, lambda e, x=src_ap: e.activation(
                scjunk[:, 0:512], x[:, 0:512], ACT.Square, bias=0.0,
                scale=1.0), reads=rc_, writes=("scjunk",))
            em.op(SC, lambda e, x=src_ap: e.activation(
                scjunk[:, 512:1024], x[:, 512:1024], ACT.Square, bias=0.0,
                scale=1.0), reads=rc_, writes=("scjunk",))
            em.op(DV, lambda e: e.reduce_sum(st[:, 1:2], scjunk[:, :], AX),
                  reads=("scjunk",), writes=("st",))
        else:
            em.op(SC, lambda e, x=src_ap: e.activation(
                scjunk[:, :], x, ACT.Copy, bias=0.0, scale=1.0,
                accum_out=st[:, 0:1]),
                reads=rc_, writes=("scjunk", "st"))
            em.op(SC, lambda e, x=src_ap: e.activation(
                scjunk[:, :], x, ACT.Square, bias=0.0, scale=1.0,
                accum_out=st[:, 1:2]),
                reads=rc_, writes=("scjunk", "st"))
        em.op(DV, lambda e: e.tensor_scalar(st[:, 2:3], st[:, 0:1], 1.0 / D,
                                            None, ALU.mult),
              reads=("st",), writes=("st",))
        em.op(DV, lambda e: e.tensor_mul(st[:, 4:5], st[:, 2:3], st[:, 2:3]),
              reads=("st",), writes=("st",))
        em.op(DV, lambda e: e.scalar_tensor_tensor(
            st[:, 3:4], st[:, 1:2], 1.0 / D, st[:, 4:5],
            ALU.mult, ALU.subtract),
            reads=("st",), writes=("st",))
        em.op(SC, lambda e: e.activation(st[:, 5:6], st[:, 3:4], ACT.Sqrt,
                                         bias=EPS, scale=1.0),
              reads=("st", "consts"), writes=("st",))
        em.op(DV, lambda e: e.reciprocal(st[:, 6:7], st[:, 5:6]),
              reads=("st",), writes=("st",))

    for dst, src in ((idb, idb_h), (idf, idf_h), (msk, mk_h), (onf, onf_h),
                     (on8, on8_h), (onc, onc_h), (cst, cst_h),
                     (bqk, bqk_h), (cbt, cb_h)):
        em.dma(SY, dst[:, :], src[:, :], writes=("consts",))
    em.dma(SY, cwt[:, :], cw_h[:, :], writes=("cwt",))

    em.op(DV, lambda e: e.memset(G[:, 0:1], 1.0),
          reads=("consts",), writes=("G",))
    em.op(DV, lambda e: e.memset(G[:, 1:S], GAMMA),
          reads=("G",), writes=("G",))
    em.op(DV, lambda e: e.tensor_tensor_scan(
        G[:, 1:S], G[:, 1:S], G[:, 1:S], 1.0, ALU.mult, ALU.bypass),
        reads=("G",), writes=("G",))
    em.op(DV, lambda e: e.reciprocal(Gi[:, :], G[:, :]),
          reads=("G",), writes=("Gi",))

    for wi, wh in enumerate((wq_h, wk_h, wv_h)):
        em.dma(GP, wring[:, :], wh[:, :], writes=("wring0", "wring1"))
        for hc in range(HC):
            for oc in range(HC):
                src = wring[:, oc * D + hc * P: oc * D + (hc + 1) * P]
                dst = tbf[:, oc * P:(oc + 1) * P]
                pe_transpose(src, dst,
                             reads=("wring0", "wring1"), writes=("tbf",))
            em.dma(SY, scr_ap(aw_s, wi, HC * D, hc * D, D), tbf[:, :],
                   reads=("tbf",))

    def w_mean_pass(wh, ncols, col0):
        nchunk = ncols // D
        for ich in range(nchunk):
            xres = "xt%d" % (ich % 2)
            xsl = xtile[:, (ich % 2) * D:(ich % 2) * D + D]
            em.dma(SY, xsl, wh[:, ich * D:(ich + 1) * D], writes=(xres,))
            cdst = col0 + 1 + (ich % 8)
            if ich < 8:
                em.op(DV, lambda e, x=xsl, c=cdst: e.reduce_sum(
                    st[:, c:c + 1], x, AX, apply_absolute_value=True),
                    reads=(xres, "st"), writes=("st",))
            else:
                em.op(DV, lambda e, x=xsl: e.reduce_sum(
                    st[:, col0:col0 + 1], x, AX, apply_absolute_value=True),
                    reads=(xres, "st"), writes=("st",))
                em.op(DV, lambda e, c=cdst: e.tensor_add(
                    st[:, c:c + 1], st[:, c:c + 1], st[:, col0:col0 + 1]),
                    reads=("st",), writes=("st",))
        for stp in (4, 2, 1):
            for k in range(stp):
                em.op(DV, lambda e, a=col0 + 1 + k, b=col0 + 1 + k + stp:
                      e.tensor_add(st[:, a:a + 1], st[:, a:a + 1],
                                   st[:, b:b + 1]),
                      reads=("st",), writes=("st",))

    def w_scale_finish(col0, ps_col, sw_col, nmean_col):
        em.op(TE, lambda e: e.matmul(pS[0:1, ps_col:ps_col + 1],
                                     st[:, col0 + 1:col0 + 2], onc[:, :],
                                     start=True, stop=True),
              reads=("st", "consts"), writes=("pSB",))
        em.op(DV, lambda e: e.tensor_scalar(
            st[0:1, col0:col0 + 1], pS[0:1, ps_col:ps_col + 1],
            1.0 / (DH * D), 1e-5, ALU.mult, ALU.max),
            reads=("pSB",), writes=("st",))
        em.op(TE, lambda e: e.matmul(pS[:, ps_col + 1:ps_col + 2],
                                     onf[0:1, :], st[0:1, col0:col0 + 1],
                                     start=True, stop=True),
              reads=("st", "consts"), writes=("pSB",))
        em.op(DV, lambda e, c=nmean_col: e.tensor_scalar(
            st[:, c:c + 1], pS[:, ps_col + 1:ps_col + 2], -1.0, None,
            ALU.mult), reads=("pSB",), writes=("st",))
        em.op(DV, lambda e, c=nmean_col, o=sw_col: e.reciprocal(
            st[:, o:o + 1], st[:, c:c + 1]), reads=("st",), writes=("st",))
        em.op(DV, lambda e, o=sw_col: e.tensor_scalar(
            st[:, o:o + 1], st[:, o:o + 1], -1.0, None, ALU.mult),
            reads=("st",), writes=("st",))

    def tern_chunk(wh, ich, sw_col):
        xres = "xt%d" % (ich % 2)
        xsl = xtile[:, (ich % 2) * D:(ich % 2) * D + D]
        em.dma(SY, xsl, wh[:, ich * D:(ich + 1) * D], writes=(xres,))
        em.op(DV, lambda e, x=xsl, c=sw_col: e.tensor_scalar(
            scr_a[:, :], x, st[:, c:c + 1], None, ALU.mult),
            reads=(xres, "st"), writes=("scr_a",))
        em.op(DV, lambda e: e.tensor_single_scalar(
            scr_b[:, :], scr_a[:, :], 0.5, ALU.is_ge),
            reads=("scr_a",), writes=("scr_b",))
        em.op(DV, lambda e: e.scalar_tensor_tensor(
            tbf[:, :], scr_a[:, :], -0.5, scr_b[:, :],
            ALU.is_lt, ALU.subtract),
            reads=("scr_a", "scr_b"), writes=("tbf",))

    w_mean_pass(w1_h, CC * D, 16)
    w_scale_finish(16, 896, 22, 23)
    em.op(DV, lambda e: e.tensor_scalar(
        cwt[:, :], cwt[:, :], st[:, 23:24], None, ALU.mult),
        reads=("cwt", "st"), writes=("cwt",))
    for cc in range(CC):
        tern_chunk(w1_h, cc, 22)
        for dc in range(HC):
            src = tbf[:, dc * P:(dc + 1) * P]
            dst = qTw[:, dc * 512 + (cc % 4) * P: dc * 512 + (cc % 4 + 1) * P]
            pe_transpose(src, dst,
                         reads=("tbf",), writes=("qTw",))
        if cc % 4 == 3:
            em.dma(SY, scr_ap(w1_s, cc // 4, HC * 512), qTw[:, 0:HC * 512],
                   reads=("qTw",))

    w_mean_pass(w2_h, HC * DH, 32)
    w_scale_finish(32, 900, 40, 41)
    for dc in range(HC):
        for g in range(DH // D):
            tern_chunk(w2_h, dc * 4 + g, 40)
            for k in range(8):
                cc = g * 8 + k
                src = tbf[:, k * P:(k + 1) * P]
                dst = qTw[:, cc * P:(cc + 1) * P]
                pe_transpose(src, dst,
                             reads=("tbf",), writes=("qTw",))
        em.dma(SY, scr_ap(w2_s, dc, CC * P), qTw[:, 0:CC * P],
               reads=("qTw",))

    for sq in range(NSEQ):
        em.region = "attn"
        for w in range(NW):
            w0 = w * WIN
            for tt in range(WT):
                ti = w0 // P + tt
                xres = "xt%d" % (tt % 2)
                xs = xtile[:, (tt % 2) * D:(tt % 2 + 1) * D]
                em.dma(SY, xs, x_h[sq, ti * P:(ti + 1) * P, :],
                       writes=(xres,))
                ln_stats(xs, xres)
                em.op(DV, lambda e, x=xs: e.tensor_scalar(
                    tbf[:, :], x, st[:, 2:3], st[:, 6:7],
                    ALU.subtract, ALU.mult),
                    reads=(xres, "st"), writes=("tbf",))
                for hc in range(HC):
                    src = tbf[:, hc * P:(hc + 1) * P]
                    dst = xlT[:, hc * WIN + tt * P: hc * WIN + (tt + 1) * P]
                    pe_transpose(src, dst,
                                 reads=("tbf",), writes=("xlT",))
            for wi in range(3):
                ld = em.dma(SY, wring[:, 0:HC * D], scr_ap(aw_s, wi, HC * D),
                            writes=("wring0", "wring1"))
                if wi < 2:
                    gtab = G if wi == 0 else Gi
                    gres = "G" if wi == 0 else "Gi"
                    boff = 0 if wi == 0 else 8
                    for oc in range(HC):
                        ps, pres = (pA, "pA") if oc % 2 == 0 else (pB, "pB")
                        ps_ap = ps[:, 0:WIN]
                        for hc in range(HC):
                            lhs = wring[:, hc * D + oc * P:
                                        hc * D + (oc + 1) * P]
                            rhs = xlT[:, hc * WIN:(hc + 1) * WIN]
                            em.op(TE, lambda e, o=ps_ap, l=lhs, r=rhs, h=hc:
                                  e.matmul(o, l, r, start=(h == 0),
                                           stop=(h == HC - 1)),
                                  reads=("wring0", "wring1", "xlT"),
                                  writes=(pres,), waits=[ld])
                        if wi == 0:
                            dst, dres = qTw[:, oc * WIN:(oc + 1) * WIN], "qTw"
                        else:
                            dst = kT[:, oc * S + w0: oc * S + w0 + WIN]
                            dres = "kT"
                        gsl = gtab[:, w0:w0 + WIN]
                        bcol = bqk[:, boff + oc: boff + oc + 1]
                        em.op(DV, lambda e, d=dst, p=ps_ap, g=gsl, b=bcol:
                              e.scalar_tensor_tensor(d, p, b, g, ALU.add,
                                                     ALU.mult),
                              reads=(pres, gres, "consts"), writes=(dres,))
                else:
                    for tt in range(WT):
                        ti = w0 // P + tt
                        for half in range(2):
                            ps, pres = (pA, "pA") if half == 0 else (pB, "pB")
                            ps_ap = ps[:, 0:512]
                            for hc in range(HC):
                                lhs = xlT[:, hc * WIN + tt * P:
                                          hc * WIN + (tt + 1) * P]
                                rhs = wring[:, hc * D + half * 512:
                                            hc * D + (half + 1) * 512]
                                em.op(TE, lambda e, o=ps_ap, l=lhs, r=rhs,
                                      h=hc: e.matmul(o, l, r,
                                                     start=(h == 0),
                                                     stop=(h == HC - 1)),
                                      reads=("wring0", "wring1", "xlT"),
                                      writes=(pres,), waits=[ld])
                            dst = vT[:, ti * D + half * 512:
                                     ti * D + (half + 1) * 512]
                            em.op(SC, lambda e, d=dst, p=ps_ap:
                                  e.activation(d, p, ACT.Copy, bias=0.0,
                                               scale=1.0),
                                  reads=(pres, "consts"), writes=("vT",))
            njt = w0 // P + WT
            for j in range(njt):
                ps, pres = (pA, "pA") if j % 2 == 0 else (pB, "pB")
                lo = max(0, j * P - w0)
                ncol = WIN - lo
                ps_ap = ps[:, 0:ncol]
                for hc in range(HC):
                    lhs = kT[:, hc * S + j * P: hc * S + (j + 1) * P]
                    rhs = qTw[:, hc * WIN + lo:(hc + 1) * WIN]
                    em.op(TE, lambda e, o=ps_ap, l=lhs, r=rhs, h=hc: e.matmul(
                        o, l, r, start=(h == 0), stop=(h == HC - 1)),
                        reads=("kT", "qTw"), writes=(pres,))
                sres = "hq%d" % j
                if j * P >= w0:
                    m_dst = hq[:, j * 512 + lo: j * 512 + lo + P]
                    em.op(DV, lambda e, d=m_dst, p=ps[:, 0:P]: e.tensor_mul(
                        d, p, msk[:, :]),
                        reads=(pres, "consts"), writes=(sres,))
                    if ncol > P:
                        r_dst = hq[:, j * 512 + lo + P: j * 512 + WIN]
                        em.op(SC, lambda e, d=r_dst, p=ps[:, P:ncol]:
                              e.activation(d, p, ACT.Copy, bias=0.0,
                                           scale=1.0),
                              reads=(pres, "consts"), writes=(sres,))
                else:
                    strip = hq[:, j * 512 + lo: j * 512 + WIN]
                    em.op(SC, lambda e, d=strip, p=ps_ap:
                          e.activation(d, p, ACT.Copy, bias=0.0, scale=1.0),
                          reads=(pres, "consts"), writes=(sres,))
            for tt in range(WT):
                i = w0 // P + tt
                xres = "xt%d" % (tt % 2)
                xs = xtile[:, (tt % 2) * D:(tt % 2 + 1) * D]
                em.dma(SY, xs, x_h[sq, i * P:(i + 1) * P, :], writes=(xres,))
                for j in range(i + 1):
                    lhs = hq[:, j * 512 + tt * P: j * 512 + (tt + 1) * P]
                    for half in range(2):
                        ps_ap = pO[:, half * 512:(half + 1) * 512]
                        pres = "pOl" if half == 0 else "pOh"
                        rhs = vT[:, j * D + half * 512:
                                 j * D + (half + 1) * 512]
                        em.op(TE, lambda e, o=ps_ap, l=lhs, r=rhs, jj=j,
                              ii=i: e.matmul(o, l, r, start=(jj == 0),
                                             stop=(jj == ii)),
                              reads=("hq%d" % j, "vT"), writes=(pres,))
                ln_stats(pO[:, :], "pOl", psum_src=True,
                         extra_reads=("pOh",))
                em.op(DV, lambda e: e.tensor_scalar(
                    scr_a[:, :], pO[:, :], st[:, 2:3], st[:, 6:7],
                    ALU.subtract, ALU.mult),
                    reads=("pOl", "pOh", "st"), writes=("scr_a",))
                dst = x2[:, i * D:(i + 1) * D]
                em.op(DV, lambda e, d=dst, x=xs: e.tensor_add(
                    d, scr_a[:, :], x),
                    reads=("scr_a", xres), writes=("x2",))
        em.region = "ffn"
        for i in range(NT):
            xsl = x2[:, i * D:(i + 1) * D]
            ln_stats(xsl, "x2")
            em.op(DV, lambda e, x=xsl: e.tensor_scalar(
                scr_a[:, :], x, st[:, 2:3], st[:, 6:7],
                ALU.subtract, ALU.mult),
                reads=("x2", "st"), writes=("scr_a",))
            em.op(DV, lambda e: e.reduce_max(st[:, 7:8], scr_a[:, :], AX,
                                             apply_absolute_value=True),
                  reads=("scr_a",), writes=("st",))
            em.op(DV, lambda e: e.tensor_single_scalar(
                st[:, 7:8], st[:, 7:8], 1e-5, ALU.max),
                reads=("st",), writes=("st",))
            em.op(DV, lambda e: e.reciprocal(st[:, 8:9], st[:, 7:8]),
                  reads=("st",), writes=("st",))
            em.op(DV, lambda e: e.tensor_single_scalar(
                st[:, 8:9], st[:, 8:9], 127.0, ALU.mult),
                reads=("st",), writes=("st",))
            em.op(DV, lambda e: e.reciprocal(st[:, 9:10], st[:, 8:9]),
                  reads=("st",), writes=("st",))
            em.op(SC, lambda e: e.activation(
                scr_b[:, :], scr_a[:, :], ACT.Copy, bias=0.0,
                scale=st[:, 8:9]),
                reads=("scr_a", "st", "consts"), writes=("scr_b",))
            em.op(DV, lambda e: e.tensor_scalar(
                u32[:, :], scr_b[:, :], 0.5, None, ALU.add),
                reads=("scr_b",), writes=("u32",))
            em.op(DV, lambda e: e.tensor_copy(scr_c[:, :], u32[:, :]),
                  reads=("u32",), writes=("scr_c",))
            em.op(DV, lambda e: e.scalar_tensor_tensor(
                scr_d[:, :], scr_b[:, :], 0.5, scr_c[:, :],
                ALU.add, ALU.is_lt),
                reads=("scr_b", "scr_c"), writes=("scr_d",))
            em.op(DV, lambda e: e.scalar_tensor_tensor(
                scr_b[:, :], scr_d[:, :], -1.0, scr_c[:, :],
                ALU.mult, ALU.add),
                reads=("scr_d", "scr_c"), writes=("scr_b",))
            em.op(DV, lambda e: e.tensor_scalar(
                tbf[:, :], scr_b[:, :], st[:, 9:10], None, ALU.mult),
                reads=("scr_b", "st"), writes=("tbf",))
            for hc in range(HC):
                src = tbf[:, hc * P:(hc + 1) * P]
                dst = kT[:, hc * S + i * P: hc * S + (i + 1) * P]
                pe_transpose(src, dst,
                             reads=("tbf",), writes=("kT",))
        for b in range(NBLK):
            em.region = "blk"
            t0 = b * BLK
            lo = t0 - 1
            clo, chi = max(lo, 0), min(t0 + BLK + 1, S)
            mid = t0 + BLK // 2
            lenA, lenB = mid - clo, chi - mid
            for cc in range(CC):
                ccg, sl4 = cc // 4, cc % 4
                if sl4 == 0:
                    w1ld = em.dma(SY, wring[:, 0:HC * 512],
                                  scr_ap(w1_s, ccg, HC * 512),
                                  writes=("wring0",))
                hres = "hs%d" % (cc % 2)
                hs = hstr[:, (cc % 2) * 516:(cc % 2) * 516 + 516]
                for (ps, pres, c0, ln) in ((pA, "pA", clo, lenA),
                                           (pB, "pB", mid, lenB)):
                    ps_ap = ps[:, 0:ln]
                    for dc in range(HC):
                        lhs = wring[:, dc * 512 + sl4 * P:
                                    dc * 512 + (sl4 + 1) * P]
                        rhs = kT[:, dc * S + c0: dc * S + c0 + ln]
                        em.op(TE, lambda e, o=ps_ap, l=lhs, r=rhs, h=dc:
                              e.matmul(o, l, r, start=(h == 0),
                                       stop=(h == HC - 1)),
                              reads=("wring0", "kT"), writes=(pres,),
                              waits=[w1ld])
                if clo > lo:
                    em.op(DV, lambda e, h=hs: e.memset(h[:, 0:1], 0.0),
                          reads=(), writes=(hres,))
                if chi < t0 + BLK + 1:
                    em.op(DV, lambda e, h=hs, c0_=chi - lo: e.memset(
                        h[:, c0_:BLK + 2], 0.0), reads=(), writes=(hres,))
                d_a = hs[:, clo - lo: clo - lo + lenA]
                d_b = hs[:, mid - lo: mid - lo + lenB]
                em.op(SC, lambda e, d=d_a, p=pA[:, 0:lenA]: e.activation(
                    d, p, ACT.Copy, bias=0.0, scale=1.0),
                    reads=("pA", "consts"), writes=(hres,))
                em.op(SC, lambda e, d=d_b, p=pB[:, 0:lenB]: e.activation(
                    d, p, ACT.Copy, bias=0.0, scale=1.0),
                    reads=("pB", "consts"), writes=(hres,))
                c0c = cwt[:, cc * 3 + 0: cc * 3 + 1]
                c1c = cwt[:, cc * 3 + 1: cc * 3 + 2]
                c2c = cwt[:, cc * 3 + 2: cc * 3 + 3]
                em.op(SC, lambda e, h=hs, c=c0c: e.activation(
                    scr_a[:, 0:BLK], h[:, 0:BLK], ACT.Copy, bias=0.0,
                    scale=c),
                    reads=(hres, "cwt", "consts"), writes=("scr_a",))
                em.op(DV, lambda e, h=hs, c=c1c: e.scalar_tensor_tensor(
                    scr_b[:, 0:BLK], h[:, 1:BLK + 1], c, scr_a[:, 0:BLK],
                    ALU.mult, ALU.add),
                    reads=(hres, "scr_a", "cwt"), writes=("scr_b",))
                em.op(DV, lambda e, h=hs, c=c2c: e.scalar_tensor_tensor(
                    scr_a[:, 0:BLK], h[:, 2:BLK + 2], c, scr_b[:, 0:BLK],
                    ALU.mult, ALU.add),
                    reads=(hres, "scr_b", "cwt"), writes=("scr_a",))
                dst = hq[:, cc * BLK:(cc + 1) * BLK]
                em.op(SC, lambda e, d=dst, c=cbt[:, cc:cc + 1]: e.activation(
                    d, scr_a[:, 0:BLK], ACT.Silu, bias=c, scale=1.0),
                    reads=("scr_a", "consts"), writes=("hq%d" % cc,))
                if cc % 8 == 7:
                    g = cc // 8
                    grp = tuple("hq%d" % c for c in range(g * 8, cc + 1))
                    rdst = scr_d if g == 0 else scr_c
                    em.op(DV, lambda e, d=rdst, g0=g * 8: e.reduce_max(
                        d[:, 0:BLK, None],
                        hq[:, g0 * BLK:(g0 + 8) * BLK].rearrange(
                            "p (c t) -> p t c", c=8), AX,
                        apply_absolute_value=True),
                        reads=grp, writes=("scr_d" if g == 0 else "scr_c",))
                    if g > 0:
                        em.op(DV, lambda e: e.tensor_max(
                            scr_d[:, 0:BLK], scr_d[:, 0:BLK],
                            scr_c[:, 0:BLK]),
                            reads=("scr_c", "scr_d"), writes=("scr_d",))
            em.op(DV, lambda e: e.tensor_copy(M_s[:, 0:BLK], scr_d[:, 0:BLK]),
                  reads=("scr_d",), writes=("M",))
            for q4 in range(BT):
                src = M_s[:, q4 * P:(q4 + 1) * P]
                sl = tp_slot[0]
                tp_slot[0] ^= 1
                ps_ap = pS[:, sl * 512: sl * 512 + P]
                pres = "pSA" if sl == 0 else "pSB"
                em.op(TE, lambda e, s=src, p=ps_ap: e.matmul(
                    p, s, idb[:, :], start=True, stop=True),
                    reads=("M", "consts"), writes=(pres,))
                em.op(DV, lambda e, p=ps_ap, q=q4: e.reduce_max(
                    st[:, 10 + q:11 + q], p, AX),
                    reads=(pres,), writes=("st",))
            em.op(DV, lambda e: e.tensor_single_scalar(
                st[:, 10:10 + BT], st[:, 10:10 + BT], 1e-5, ALU.max),
                reads=("st",), writes=("st",))
            em.op(DV, lambda e: e.reciprocal(st[:, 14:14 + BT],
                                             st[:, 10:10 + BT]),
                  reads=("st",), writes=("st",))
            em.op(DV, lambda e: e.tensor_single_scalar(
                st[:, 14:14 + BT], st[:, 14:14 + BT], 127.0, ALU.mult),
                reads=("st",), writes=("st",))
            em.op(DV, lambda e: e.reciprocal(st[:, 18:18 + BT],
                                             st[:, 14:14 + BT]),
                  reads=("st",), writes=("st",))
            em.op(DV, lambda e: e.tensor_scalar(
                st[:, 44:44 + BT], st[:, 18:18 + BT], st[:, 41:42],
                None, ALU.mult), reads=("st",), writes=("st",))
            sl = tp_slot[0]
            tp_slot[0] ^= 1
            ps_ap = pS[0:BT, sl * 512: sl * 512 + P]
            pres = "pSA" if sl == 0 else "pSB"
            em.op(TE, lambda e, p=ps_ap: e.matmul(
                p, st[:, 14:14 + BT], idf[:, :], start=True, stop=True),
                reads=("st", "consts"), writes=(pres,))
            em.op(DV, lambda e, p=ps_ap: e.tensor_copy(sxT[0:BT, :], p),
                  reads=(pres,), writes=("sxT",))
            for q4 in range(BT):
                ps_r = pS[:, 768:768 + P]
                em.op(DV, lambda e, r=q4: e.tensor_scalar(
                    sxm[0:BT, :], sxT[0:BT, :],
                    idf[0:BT, r:r + 1], None, ALU.mult),
                    reads=("sxT", "consts"), writes=("sxm",))
                em.op(TE, lambda e, p=ps_r: e.matmul(
                    p, on8[0:BT, :], sxm[0:BT, :],
                    start=True, stop=True),
                    reads=("sxm", "consts"), writes=("pSB",))
                em.op(SC, lambda e, p=ps_r, q=q4:
                      e.activation(Row[:, q * P:(q + 1) * P], p, ACT.Copy,
                                   bias=0.0, scale=1.0),
                      reads=("pSB", "consts"), writes=("Row",))
            for cc in range(CC):
                hres = "hq%d" % cc
                ssl = hq[:, cc * BLK:(cc + 1) * BLK]
                em.op(DV, lambda e, s=ssl: e.tensor_mul(
                    scr_a[:, 0:BLK], s, Row[:, 0:BLK]),
                    reads=(hres, "Row"), writes=("scr_a",))
                em.op(DV, lambda e: e.tensor_scalar(
                    u32[:, 0:BLK], scr_a[:, 0:BLK], 0.5, None, ALU.add),
                    reads=("scr_a",), writes=("u32",))
                em.op(DV, lambda e: e.tensor_copy(scr_c[:, 0:BLK],
                                                  u32[:, 0:BLK]),
                      reads=("u32",), writes=("scr_c",))
                em.op(DV, lambda e: e.scalar_tensor_tensor(
                    scr_b[:, 0:BLK], scr_a[:, 0:BLK], 0.5, scr_c[:, 0:BLK],
                    ALU.add, ALU.is_lt),
                    reads=("scr_a", "scr_c"), writes=("scr_b",))
                em.op(DV, lambda e, s=ssl: e.scalar_tensor_tensor(
                    s, scr_b[:, 0:BLK], -1.0, scr_c[:, 0:BLK],
                    ALU.mult, ALU.add),
                    reads=("scr_b", "scr_c"), writes=(hres,))
            psums = [(pA[:, 0:BLK], "pA"), (pB[:, 0:BLK], "pB"),
                     (pO[:, 0:BLK], "pOl"), (pO[:, 512:512 + BLK], "pOh"),
                     (pP[:, 0:BLK], "pPl"), (pP[:, 512:512 + BLK], "pPh"),
                     (pS[:, 0:BLK], "pSA"), (pS[:, 512:512 + BLK], "pSB")]
            for dc in range(HC):
                w2ld = em.dma(SY, wring[:, 4096:4096 + CC * P],
                              scr_ap(w2_s, dc, CC * P), writes=("wring1",))
                ps_ap, pres = psums[dc]
                for cc in range(CC):
                    lhs = wring[:, 4096 + cc * P:4096 + (cc + 1) * P]
                    rhs = hq[:, cc * BLK:(cc + 1) * BLK]
                    em.op(TE, lambda e, o=ps_ap, l=lhs, r=rhs, c=cc:
                          e.matmul(o, l, r, start=(c == 0),
                                   stop=(c == CC - 1)),
                          reads=("wring1", "hq%d" % cc), writes=(pres,),
                          waits=[w2ld])
            for dc in range(HC):
                em.op(SC, lambda e, d=hq[:, dc * BLK:(dc + 1) * BLK],
                      p=psums[dc][0]:
                      e.activation(d, p, ACT.Copy, bias=0.0, scale=1.0),
                      reads=(psums[dc][1], "consts"),
                      writes=("y2s", "hq%d" % dc))
            for q4 in range(BT):
                i = t0 // P + q4
                for dc in range(HC):
                    src = hq[:, dc * BLK + q4 * P: dc * BLK + (q4 + 1) * P]
                    dst = scr_d[:, dc * P:(dc + 1) * P]
                    pe_transpose(src, dst, reads=("y2s",), writes=("scr_d",),
                                 scale=st[:, 44 + q4:45 + q4], scres=("st",))
                ln_stats(scr_d[:, :], "scr_d")
                em.op(DV, lambda e: e.tensor_scalar(
                    scr_a[:, :], scr_d[:, :], st[:, 2:3], st[:, 6:7],
                    ALU.subtract, ALU.mult),
                    reads=("scr_d", "st"), writes=("scr_a",))
                em.op(DV, lambda e, x=x2[:, i * D:(i + 1) * D]:
                      e.tensor_add(scr_b[:, :], scr_a[:, :], x),
                      reads=("scr_a", "x2"), writes=("scr_b",))
                em.dma(GP, y_h[sq, i * P:(i + 1) * P, :], scr_b[:, :],
                       reads=("scr_b",))

    em._raw_op(SY, None, em.dma_all(SC) + em.dma_all(SY) + em.dma_all(GP),
               ())

    block = ctx.enter_context(nc.Block())
    with nc.allow_low_precision(reason="bf16 matmul staging; fp32 accum"):
        em.emit(block)
    ctx.close()
    return nc


def _host_pack(Wq, bq, Wk, bk, Wv, w1, conv_w, conv_b, w2):
    f32 = np.float32
    pk = {}
    pk["wq"] = np.ascontiguousarray(
        np.asarray(Wq, f32).reshape(HC, P, D).transpose(1, 0, 2)
        .reshape(P, HC * D))
    pk["wk"] = np.ascontiguousarray(
        np.asarray(Wk, f32).reshape(HC, P, D).transpose(1, 0, 2)
        .reshape(P, HC * D))
    pk["wv"] = np.ascontiguousarray(
        np.asarray(Wv, f32).reshape(HC, P, D).transpose(1, 0, 2)
        .reshape(P, HC * D))
    bqk = np.zeros((P, 16), f32)
    bqk[:, 0:8] = np.asarray(bq, f32).reshape(HC, P).T
    bqk[:, 8:16] = np.asarray(bk, f32).reshape(HC, P).T
    pk["bqk"] = bqk
    pk["w1"] = np.ascontiguousarray(
        np.asarray(w1, f32).reshape(CC, P, D).transpose(1, 0, 2)
        .reshape(P, CC * D))
    pk["w2"] = np.ascontiguousarray(
        np.asarray(w2, f32).reshape(HC, P, DH).transpose(1, 0, 2)
        .reshape(P, HC * DH))
    pk["cw"] = np.ascontiguousarray(
        np.asarray(conv_w, f32).reshape(CC, P, 3).transpose(1, 0, 2)
        .reshape(P, CC * 3))
    pk["cb"] = np.ascontiguousarray(np.asarray(conv_b, f32).reshape(CC, P).T)
    pk["idb"] = np.eye(P, dtype=ml_dtypes.bfloat16)
    pk["idf"] = np.eye(P, dtype=np.float32)
    pk["msk"] = (np.arange(P)[None, :] >= np.arange(P)[:, None]).astype(
        ml_dtypes.bfloat16)
    pk["onf"] = np.ones((1, P), np.float32)
    pk["on8"] = np.ones((8, P), np.float32)
    pk["onc"] = np.ones((P, 1), np.float32)
    pk["cst"] = np.tile(np.array([[EPS, 0.0, 1.0, 0, 0, 0, 0, 0]],
                                 np.float32), (P, 1))
    return pk


def kernel(L0_attn_x, L0_attn_Wq, L0_attn_bq, L0_attn_Wk, L0_attn_bk,
           L0_attn_Wv, L0_ffn_w1, L0_ffn_conv_w, L0_ffn_conv_b, L0_ffn_w2):
    B, S, _ = L0_attn_x.shape
    NSEQ = B // NCORES
    pk = _host_pack(L0_attn_Wq, L0_attn_bq, L0_attn_Wk, L0_attn_bk,
                    L0_attn_Wv, L0_ffn_w1, L0_ffn_conv_w, L0_ffn_conv_b,
                    L0_ffn_w2)
    nc = build_nc(NSEQ, S, xdv=())
    x = np.ascontiguousarray(np.asarray(L0_attn_x, np.float32))
    in_maps = []
    for c in range(NCORES):
        m = dict(pk)
        m["x"] = x[c * NSEQ:(c + 1) * NSEQ]
        in_maps.append(m)
    res = run_bass_kernel_spmd(nc, in_maps, core_ids=list(range(NCORES)))
    out = np.concatenate([np.asarray(r["y"]) for r in res.results], axis=0)
    return out.astype(np.float32, copy=False)
